# version 02
# baseline (speedup 1.0000x reference)
import numpy as np
import ml_dtypes
import concourse.bass as bass
import concourse.mybir as mybir
from concourse.bass_utils import run_bass_kernel_spmd

F32 = mybir.dt.float32
BF16 = mybir.dt.bfloat16
AF = mybir.ActivationFunctionType
ALU = mybir.AluOpType
AX = mybir.AxisListType

D = 1024
L = 16384
NB = 2
T = NB * L
NC = 8
TQ = T // NC
E = 2048
S = 128
FE, H = 8, 2048
F = 2 * E + S
EPS = 1e-5
INVL = 1.0 / L
NT = L // 128
NOWN = TQ // 128

QB = 512
NQB = TQ // QB
SUP = 2048
NSUP = L // SUP
KKS = SUP // 128

DBG = False

COUNT = "count"
QSEM = {"sync": "QS", "scalar": "QA", "gpsimd": "QG"}
ESEM = {"tensor": "TE", "vector": "VE", "scalar": "SE", "gpsimd": "GE", "sync": "SY"}


class Bld:
    def __init__(self, nc):
        self.nc = nc
        self.sems = {n: nc.alloc_semaphore(n) for n in ("QS", "QA", "QG", "TE", "VE", "SE", "GE", "SY")}
        self.labels = {}
        self.cnt = {k: 0 for k in self.sems}
        self.final = None
        self.mode = COUNT
        self.eng = None
        self.engname = None

    def start_pass(self, mode, eng=None):
        self.mode = mode
        self.eng = eng
        self.engname = None if mode == COUNT else mode

    def _wait1(self, label):
        force = False
        if isinstance(label, tuple) and len(label) == 2 and label[0] == "!":
            force = True
            label = label[1]
        s, v, peng = self.labels[label]
        if peng == self.engname and not force:
            return
        self.eng.wait_ge(self.sems[s], v)

    def _lab(self, label, semname, q):
        for lb in (label if isinstance(label, list) else [label]):
            assert lb not in self.labels, f"duplicate label {lb}"
            self.labels[lb] = (semname, self.cnt[semname], q)

    def dma(self, q, out, in_, label=None, waits=(), **kw):
        sn = QSEM[q]
        if self.mode == COUNT:
            self.cnt[sn] += 16
            if label is not None:
                self._lab(label, sn, q)
        elif self.engname == q:
            for w in waits:
                self._wait1(w)
            self.eng.dma_start(out=out, in_=in_, **kw).then_inc(self.sems[sn], 16)

    def op(self, engname, emit, label=None, waits=()):
        sn = ESEM[engname]
        if self.mode == COUNT:
            if label is not None:
                self.cnt[sn] += 1
                self._lab(label, sn, engname)
        elif self.engname == engname:
            for w in waits:
                self._wait1(w)
            i = emit(self.eng)
            if label is not None:
                i.then_inc(self.sems[sn], 1)

    def wait(self, engname, *labels):
        if self.mode == engname:
            for w in labels:
                self._wait1(w)

    def wait_count(self, engname, semname):
        if self.mode == engname:
            self.eng.wait_ge(self.sems[semname], self.final[semname])


def build(stop_after=3, debug=False):
    nc = bass.Bass("TRN2", target_bir_lowering=False, debug=False)
    ap = lambda h: h.ap()

    xb = ap(nc.dram_tensor("xb", [L, D], F32, kind="ExternalInput"))
    W_in = ap(nc.dram_tensor("w_in", [D, F], F32, kind="ExternalInput"))
    b_in = ap(nc.dram_tensor("b_in", [1, F], F32, kind="ExternalInput"))
    lng_h = ap(nc.dram_tensor("lng", [128, 8], F32, kind="ExternalInput"))
    lnb_h = ap(nc.dram_tensor("lnb", [128, 8], F32, kind="ExternalInput"))
    gq_h = ap(nc.dram_tensor("gq", [128, 1], F32, kind="ExternalInput"))
    bq_h = ap(nc.dram_tensor("bq", [128, 1], F32, kind="ExternalInput"))
    gk_h = ap(nc.dram_tensor("gk", [128, 1], F32, kind="ExternalInput"))
    bk_h = ap(nc.dram_tensor("bk", [128, 1], F32, kind="ExternalInput"))
    wout_f = ap(nc.dram_tensor("wout", [E, D], F32, kind="ExternalInput"))
    bout_f = ap(nc.dram_tensor("bout", [1, D], F32, kind="ExternalInput"))
    gwT_h = ap(nc.dram_tensor("gwt", [D, FE], F32, kind="ExternalInput"))
    gb_h = ap(nc.dram_tensor("gb", [1, FE], F32, kind="ExternalInput"))
    w0T_h = ap(nc.dram_tensor("w0t", [FE, D, H], F32, kind="ExternalInput"))
    w1T_h = ap(nc.dram_tensor("w1t", [FE, D, H], F32, kind="ExternalInput"))
    w2T_h = ap(nc.dram_tensor("w2t", [FE, H, D], F32, kind="ExternalInput"))
    b0_h = ap(nc.dram_tensor("b0", [FE, 128, 16], F32, kind="ExternalInput"))
    b1_h = ap(nc.dram_tensor("b1", [FE, 128, 16], F32, kind="ExternalInput"))
    b2_h = ap(nc.dram_tensor("b2", [FE, D], F32, kind="ExternalInput"))

    O = ap(nc.dram_tensor("o", [TQ, D], F32, kind="ExternalOutput"))

    sk = "ExternalOutput" if debug else "Internal"
    v_scr = ap(nc.dram_tensor("v_scr", [L, E], BF16, kind=sk))
    uT_scr = ap(nc.dram_tensor("uT_scr", [E, TQ], F32, kind=sk))
    xnT_scr = ap(nc.dram_tensor("xnT_scr", [D, TQ], BF16, kind=sk))
    xn_scr = ap(nc.dram_tensor("xn_scr", [TQ, D], F32, kind=sk))
    s_scr = ap(nc.dram_tensor("s_scr", [TQ, D], F32, kind=sk))
    w0b = ap(nc.dram_tensor("w0b", [FE, D, H], BF16, kind="Internal"))
    w1b = ap(nc.dram_tensor("w1b", [FE, D, H], BF16, kind="Internal"))
    w2b = ap(nc.dram_tensor("w2b", [FE, H, D], BF16, kind="Internal"))
    woutb = ap(nc.dram_tensor("woutb", [E, D], BF16, kind="Internal"))
    boutb = ap(nc.dram_tensor("boutb", [1, D], BF16, kind="Internal"))
    b2b = ap(nc.dram_tensor("b2b", [FE, D], BF16, kind="Internal"))

    identb_h = ap(nc.inline_tensor(np.eye(128, dtype=ml_dtypes.bfloat16), name="identb_c"))
    identf_h = ap(nc.inline_tensor(np.eye(128, dtype=np.float32), name="identf_c"))

    if debug:
        d_kT = ap(nc.dram_tensor("d_kT", [S, L], BF16, kind="ExternalOutput"))
        d_qT = ap(nc.dram_tensor("d_qT", [S, TQ], BF16, kind="ExternalOutput"))
        d_w8 = ap(nc.dram_tensor("d_w8", [128, NOWN * 8], F32, kind="ExternalOutput"))
        d_st = ap(nc.dram_tensor("d_st", [128, 32], F32, kind="ExternalOutput"))
        d_bin = ap(nc.dram_tensor("d_bin", [1, F], BF16, kind="ExternalOutput"))

    base = (nc.sbuf_base + 31) & ~31
    top = nc.sbuf_top

    class Alloc:
        def __init__(self, start):
            self.off = start

        def t(self, name, shape, dt):
            sz = int(np.prod(shape[1:])) * (2 if dt == BF16 else 4)
            o = (self.off + 31) & ~31
            h = nc.alloc_sbuf_tensor_at(name, shape, dt, offset=o)
            self.off = o + sz
            assert self.off <= top, f"SBUF overflow at {name}: {(self.off - base)} bytes"
            return h.ap()

    pa = Alloc(base)
    identb = pa.t("identb", [128, 128], BF16)
    identf = pa.t("identf", [128, 128], F32)
    ones_bf = pa.t("ones_bf", [1, 512], BF16)
    ones_f = pa.t("ones_f", [1, 128], F32)
    gq_c = pa.t("gq_c", [128, 1], F32)
    bq_c = pa.t("bq_c", [128, 1], F32)
    gk_c = pa.t("gk_c", [128, 1], F32)
    bk_c = pa.t("bk_c", [128, 1], F32)
    gqp = pa.t("gqp", [128, 1], F32)
    bqp = pa.t("bqp", [128, 1], F32)
    lng = pa.t("lng_s", [128, 8], F32)
    lnb = pa.t("lnb_s", [128, 8], F32)
    gwT = pa.t("gwT_s", [128, 64], F32)
    gb = pa.t("gb_s", [1, 8], F32)
    W8 = pa.t("W8", [128, NOWN * 8], F32)
    eps_c = pa.t("eps_c", [128, 1], F32)
    neg1_c = pa.t("neg1_c", [128, 1], F32)
    pers_end = pa.off

    aa = Alloc(pers_end)
    kT = aa.t("kT", [128, L], BF16)
    qT = aa.t("qT", [128, TQ], BF16)
    regA_end = aa.off

    p1 = Alloc(regA_end)
    Winp = p1.t("Winp", [128, 8 * F], BF16)
    binrow_bf = p1.t("binrow_bf", [1, F], BF16)
    stat = [p1.t(f"stat{i}", [128, 16], F32) for i in range(2)]
    zT_sb = p1.t("zT_sb", [128, 128], BF16)
    z_sb = [p1.t(f"z_sb{i}", [128, 128], BF16) for i in range(2)]
    ovl = p1.off
    x_sb = [p1.t(f"x_sb{i}", [128, D], F32) for i in range(3)]
    xn_sb = [p1.t(f"xn_sb{i}", [128, D], F32) for i in range(2)]
    xn2n_sb = [p1.t(f"xn2n_sb{i}", [128, D], BF16) for i in range(2)]
    sq_scr = p1.t("sq_scr", [128, D], F32)
    xnTf = p1.t("xnTf", [128, D], F32)
    xn2nT = [p1.t(f"xn2nT{i}", [128, 8 * 512], BF16) for i in range(2)]
    xnT_b = p1.t("xnT_b", [128, 8 * 512], BF16)
    v_sb = [p1.t(f"v_sb{i}", [128, E], BF16) for i in range(2)]
    u_sb = [p1.t(f"u_sb{i}", [128, 512], F32) for i in range(2)]

    p2 = Alloc(regA_end)
    aT = [p2.t(f"aT{i}", [128, KKS * 512], BF16) for i in range(2)]
    acc = p2.t("acc2", [128, 16 * 512], F32)
    Wout_sb = p2.t("Wout_sb", [128, 16 * D], BF16)
    vbuf = [p2.t(f"vbuf{i}", [128, 8 * 512], BF16) for i in range(2)]
    rel_sb = [p2.t(f"rel_sb{i}", [128, 512], F32) for i in range(2)]
    u2_sb = [p2.t(f"u2_sb{i}", [128, 512], F32) for i in range(2)]
    xn2_sb = [p2.t(f"xn2_sb{i}", [128, D], F32) for i in range(2)]
    s_sb = [p2.t(f"s_sb{i}", [128, D], F32) for i in range(2)]
    boutr = p2.t("boutr", [1, D], BF16)

    p3 = Alloc(pers_end)
    w0s = [p3.t(f"w0s{i}", [128, 8 * 1024], BF16) for i in range(2)]
    w1s = [p3.t(f"w1s{i}", [128, 8 * 1024], BF16) for i in range(2)]
    w2s = [p3.t(f"w2s{i}", [128, 8 * 1024], BF16) for i in range(2)]
    xnT_q = p3.t("xnT_q", [128, 8 * 1024], BF16)
    acc_q = p3.t("acc_q", [128, 8 * 1024], F32)
    actT = p3.t("actT", [128, 8 * 512], BF16)
    h0_sb = [p3.t(f"h0_sb{i}", [128, 512], F32) for i in range(2)]
    si_sb = [p3.t(f"si_sb{i}", [128, 512], F32) for i in range(2)]
    b0c = [p3.t(f"b0c{i}", [128, 8], F32) for i in range(2)]
    b1c = [p3.t(f"b1c{i}", [128, 8], F32) for i in range(2)]
    b2r = [p3.t(f"b2r{i}", [1, D], BF16) for i in range(2)]
    g_sb = [p3.t(f"g_sb{i}", [128, D], F32) for i in range(2)]
    s3_sb = [p3.t(f"s3_sb{i}", [128, D], F32) for i in range(2)]
    x3_sb = [p3.t(f"x3_sb{i}", [128, D], F32) for i in range(2)]

    P0 = nc.alloc_psum_tensor("P0", [128, 512], F32).ap()
    P1 = nc.alloc_psum_tensor("P1", [128, 512], F32).ap()
    P2 = nc.alloc_psum_tensor("P2", [128, 512], F32).ap()
    P3 = nc.alloc_psum_tensor("P3", [128, 512], F32).ap()
    PW = nc.alloc_psum_tensor("PWI", [128, 1024], F32).ap()
    PT0 = nc.alloc_psum_tensor("PT0", [128, 1024], BF16).ap()
    PT1 = nc.alloc_psum_tensor("PT1", [128, 1024], BF16).ap()
    PT = [PT0, PT1]

    b = Bld(nc)

    def prog():
        MM = lambda out, l, r, st, sp: (lambda e: e.matmul(out, l, r, start=st, stop=sp, skip_group_check=True))
        TS = lambda out, i0, s1, s2, o0, o1: (lambda e: e.tensor_scalar(out=out, in0=i0, scalar1=s1, scalar2=s2, op0=o0, op1=o1))
        TT = lambda out, i0, i1, o: (lambda e: e.tensor_tensor(out=out, in0=i0, in1=i1, op=o))
        STT = lambda out, i0, sc, i1, o0, o1: (lambda e: e.scalar_tensor_tensor(out=out, in0=i0, scalar=sc, in1=i1, op0=o0, op1=o1))
        ACT = lambda out, i, f, bias=0.0, scale=1.0, acc=None: (lambda e: e.activation(out=out, in_=i, func=f, bias=bias, scale=scale, accum_out=acc))
        CP = lambda out, i: (lambda e: e.tensor_copy(out, i))

        b.dma("sync", identb, identb_h[:, :])
        b.dma("sync", identf, identf_h[:, :])
        b.dma("sync", gq_c, gq_h[:, :])
        b.dma("sync", bq_c, bq_h[:, :])
        b.dma("sync", gk_c, gk_h[:, :])
        b.dma("sync", bk_c, bk_h[:, :])
        b.dma("sync", lng, lng_h[:, :])
        b.dma("sync", lnb, lnb_h[:, :])
        b.dma("sync", gwT.rearrange("p (c e) -> p c e", c=8),
              gwT_h.rearrange("(c p) e -> p c e", p=128), label=("pre_gwT",))
        b.dma("sync", gb, gb_h[:, :], label=("pre1",))
        b.dma("gpsimd", woutb[:, :], wout_f[:, :], label=("wcout",))
        b.dma("gpsimd", boutb[:, :], bout_f[:, :], label=("wcbout",))
        b.dma("gpsimd", b2b[:, :], b2_h[:, :], label=("wcb2",))
        b.dma("gpsimd", Winp.rearrange("p (c f) -> p c f", c=8),
              W_in.rearrange("(c p) f -> p c f", p=128), label=("lwinp",))
        b.dma("gpsimd", binrow_bf[:, :], b_in[:, :], label=("lbin",))
        for e_ in range(FE):
            for hh in range(2):
                b.dma("gpsimd", w0b[e_, :, hh * 1024:(hh + 1) * 1024], w0T_h[e_, :, hh * 1024:(hh + 1) * 1024])
                b.dma("gpsimd", w1b[e_, :, hh * 1024:(hh + 1) * 1024], w1T_h[e_, :, hh * 1024:(hh + 1) * 1024])
                b.dma("gpsimd", w2b[e_, hh * 1024:(hh + 1) * 1024, :], w2T_h[e_, hh * 1024:(hh + 1) * 1024, :],
                      label=("wc", e_, hh))
        b.op("vector", lambda e: e.memset(ones_bf[:, :], 1.0))
        b.op("vector", lambda e: e.memset(ones_f[:, :], 1.0))
        b.op("vector", lambda e: e.memset(eps_c[:, :], EPS))
        b.op("vector", lambda e: e.memset(neg1_c[:, :], -1.0))
        b.op("vector", TS(gqp, gq_c, INVL, None, ALU.mult, ALU.bypass), waits=[("pre1",)])
        b.op("vector", TS(bqp, bq_c, INVL, None, ALU.mult, ALU.bypass), label=("qscale",))

        def ztranspose(tp):
            b.op("tensor", (lambda e: e.transpose(PT[tp % 2][:, 128:256], z_sb[tp % 2], identb)),
                 label=("trz", tp),
                 waits=[("sz", tp), ("trev", tp, 7)] + ([("zev", tp - 2)] if tp >= 2 else []))
            b.op("vector", CP(zT_sb[:, :], PT[tp % 2][:, 128:256]), label=("zev", tp), waits=[("trz", tp)])
            pw_ = [("pre1",)] if tp == 0 else []
            if tp < NOWN:
                b.op("vector", TS(qT[:, tp * 128:(tp + 1) * 128], zT_sb, gqp[:, 0:1], bqp[:, 0:1],
                                  ALU.mult, ALU.add), waits=[("qscale",), ("!", ("zev", tp))] + pw_)
            b.op("vector", TS(kT[:, tp * 128:(tp + 1) * 128], zT_sb, gk_c[:, 0:1], bk_c[:, 0:1],
                              ALU.mult, ALU.add), label=("kq", tp), waits=[("!", ("zev", tp))] + pw_)

        def tile(t):
            own = t < NOWN
            grp = t // 4
            pr = t % 2
            st_ = stat[pr]
            b.dma("sync", x_sb[t % 3], xb[t * 128:(t + 1) * 128, :], label=("lx", t),
                  waits=[("xfree", t - 3), ("r1", t - 3)] if t >= 3 else [])
            sw = [("lx", t)]
            if t >= 2:
                sw.append(("xn2n", t - 2))
            if 0 <= t - 1 < NOWN:
                sw.append(("w8", t - 1))
            if t >= 1:
                sw.append(("ssq2v", t - 1))
            b.op("scalar", ACT(sq_scr[:, :], x_sb[t % 3], AF.Square),
                 label=("sq1", t), waits=sw)
            b.op("vector", (lambda e: e.reduce_sum(st_[:, 0:1], x_sb[t % 3], axis=AX.X)),
                 label=("s1v", t), waits=[("lx", t), ("sq1", t)])
            b.op("vector", (lambda e: e.reduce_sum(st_[:, 1:2], sq_scr[:, :], axis=AX.X)),
                 label=("ssq1v", t))
            b.op("vector", TS(st_[:, 2:3], st_[:, 0:1], 1.0 / D, None, ALU.mult, ALU.bypass),
                 label=("m1v", t), waits=[("!", ("s1v", t))])
            b.op("vector", TT(st_[:, 3:4], st_[:, 2:3], st_[:, 2:3], ALU.mult),
                 label=("m1sv", t), waits=[("!", ("m1v", t))])
            b.op("vector", STT(st_[:, 4:5], st_[:, 1:2], 1.0 / D, st_[:, 3:4], ALU.mult, ALU.subtract),
                 label=("v1", t), waits=[("!", ("m1sv", t)), ("!", ("ssq1v", t))])
            b.op("scalar", ACT(st_[:, 5:6], st_[:, 4:5], AF.Sqrt, bias=eps_c[:, 0:1]), label=("sd1", t), waits=[("v1", t)])
            b.op("vector", (lambda e: e.reciprocal(st_[:, 6:7], st_[:, 5:6])), label=("r1v", t), waits=[("sd1", t)])
            b.op("vector", STT(st_[:, 7:8], st_[:, 2:3], -1.0, st_[:, 6:7], ALU.mult, ALU.mult), label=("r1", t),
                 waits=[("!", ("r1v", t))])
            xnw = [("r1", t)]
            if t >= 2:
                xnw.append(("xn2n", t - 2))
            if 0 <= t - 2 < NOWN:
                xnw += [("trf", t - 2, 7), ("xnst", t - 2)]
            b.op("scalar", ACT(xn_sb[pr][:, :], x_sb[t % 3], AF.Identity, bias=st_[:, 7:8], scale=st_[:, 6:7]),
                 label=[("xn", t), ("xfree", t)], waits=xnw)
            b.op("vector", (lambda e: e.reduce_sum(st_[:, 8:9], xn_sb[pr], axis=AX.X)),
                 label=("s2v", t), waits=[("xn", t)])
            b.op("scalar", ACT(sq_scr[:, :], xn_sb[pr], AF.Square), label=("sq2", t), waits=[("ssq1v", t)])
            b.op("vector", (lambda e: e.reduce_sum(st_[:, 9:10], sq_scr[:, :], axis=AX.X)),
                 label=("ssq2v", t), waits=[("sq2", t)])
            b.op("vector", TS(st_[:, 10:11], st_[:, 8:9], 1.0 / D, None, ALU.mult, ALU.bypass),
                 label=("m2v", t), waits=[("!", ("s2v", t))])
            b.op("vector", TT(st_[:, 11:12], st_[:, 10:11], st_[:, 10:11], ALU.mult),
                 label=("m2sv", t), waits=[("!", ("m2v", t))])
            b.op("vector", STT(st_[:, 12:13], st_[:, 9:10], 1.0 / D, st_[:, 11:12], ALU.mult, ALU.subtract),
                 label=("v2", t), waits=[("!", ("m2sv", t)), ("!", ("ssq2v", t))])
            b.op("scalar", ACT(st_[:, 13:14], st_[:, 12:13], AF.Sqrt, bias=eps_c[:, 0:1]), label=("sd2", t), waits=[("v2", t)])
            b.op("vector", (lambda e: e.reciprocal(st_[:, 14:15], st_[:, 13:14])), label=("r2v", t), waits=[("sd2", t)])
            b.op("vector", STT(st_[:, 15:16], st_[:, 10:11], -1.0, st_[:, 14:15], ALU.mult, ALU.mult),
                 label=("nb2v", t), waits=[("!", ("r2v", t))])
            b.op("scalar", ACT(xn2n_sb[pr][:, :], xn_sb[pr], AF.Identity, bias=st_[:, 15:16],
                               scale=st_[:, 14:15]),
                 label=("xn2n", t), waits=[("nb2v", t)] + ([("tr", t - 2, 7)] if t >= 2 else []))
            if own:
                b.dma("sync", xn_scr[t * 128:(t + 1) * 128, :], xn_sb[pr], label=("xnst", t), waits=[("xn", t)])
            if debug and t < 2:
                b.dma("sync", d_st[:, t * 16:(t + 1) * 16], st_[:, :], label=("dst", t), waits=[("xn2n", t)])

            if t >= 1:
                ztranspose(t - 1)

            for c in range(8):
                ci = t * 8 + c
                w = [("xn2n", t)]
                if ci >= 2:
                    w.append(("trev",) + divmod(ci - 2, 8))
                if c == 0 and t >= 1:
                    w.append(("zev", t - 1))
                if c == 0 and t == 0:
                    w.append(("pre1",))
                b.op("tensor", (lambda e, c=c: e.transpose(PT[c % 2][:, 0:128],
                                                           xn2n_sb[pr][:, c * 128:(c + 1) * 128], identb)),
                     label=("tr", t, c), waits=w)
                ew = [("tr", t, c)]
                if t % 4 == 0 and c == 0:
                    if grp >= 2:
                        ew.append(("xTfree", grp - 2))
                    if t == 0:
                        ew.append(("pre1",))
                dst = xn2nT[grp % 2][:, c * 512 + (t % 4) * 128: c * 512 + (t % 4) * 128 + 128]
                b.op("vector", TS(dst, PT[c % 2][:, 0:128], lng[:, c:c + 1], lnb[:, c:c + 1],
                                  ALU.mult, ALU.add),
                     label=[("trev", t, c)] + ([("trev8", t)] if c == 7 else []), waits=ew)

            if own:
                PF = [P2, P3]
                for c in range(8):
                    w = [("xn", t)]
                    if c == 0 and t == 0:
                        w.append(("pre1",))
                    if c >= 2:
                        w.append(("trfev", t, c - 2))
                    elif t >= 1:
                        if (t - 1) % 4 == 3:
                            w.append(("su", (t - 1) // 4, 14 + c))
                        else:
                            w.append(("trfev", t - 1, 6 + c))
                    b.op("tensor", (lambda e, c=c: e.transpose(PF[c % 2][:, 0:128],
                                                               xn_sb[pr][:, c * 128:(c + 1) * 128], identf)),
                         label=("trf", t, c), waits=w)
                    ew = [("trf", t, c)]
                    if c == 0 and t >= 1:
                        ew.append(("gmm", t - 1))
                    if t % 4 == 0 and c == 0 and grp >= 1:
                        ew.append(("xTst", grp - 1))
                    b.op("vector", CP(xnTf[:, c * 128:(c + 1) * 128], PF[c % 2][:, 0:128]), waits=ew)
                    b.op("vector", CP(xnT_b[:, c * 512 + (t % 4) * 128: c * 512 + (t % 4) * 128 + 128],
                                      PF[c % 2][:, 0:128]),
                         label=[("trfev", t, c)] + ([("xnTb", t)] if c == 7 else []))

            PV = [P0, P1]
            offs = [(E + g * 512, 512) for g in range(4)] + [(2 * E, S)]
            for g, (off, wdt) in enumerate(offs):
                gi = t * 5 + g
                bank = PV[gi % 2]
                for c in range(8):
                    w = []
                    if c == 0:
                        w.append(("trev8", t))
                        if gi >= 2:
                            pt_, pg_ = divmod(gi - 2, 5)
                            w.append(("sv", pt_, pg_) if pg_ < 4 else ("sz", pt_))
                        if t == 0 and g == 0:
                            w += [("lwinp",), ("lbin",)]
                    b.op("tensor", MM(bank[:, 0:wdt],
                                      xn2nT[grp % 2][:, c * 512 + (t % 4) * 128: c * 512 + (t % 4) * 128 + 128],
                                      Winp[:, c * F + off: c * F + off + wdt], c == 0, False), waits=w)
                lab = [("vps", t, g)]
                if g == 4 and t % 4 == 3 and not own:
                    lab.append(("xTfree", grp))
                b.op("tensor", MM(bank[:, 0:wdt], ones_bf[0:1, 0:128], binrow_bf[0:1, off:off + wdt],
                                  False, True), label=lab)
                if g < 4:
                    w = [("vps", t, g)]
                    if g == 0 and t >= 2:
                        w.append(("vst", t - 2))
                    b.op("scalar", ACT(v_sb[pr][:, g * 512:(g + 1) * 512], bank[:, 0:512], AF.Silu),
                         label=("sv", t, g), waits=w)
                else:
                    w = [("vps", t, g)]
                    if t >= 2:
                        w.append(("trz", t - 2))
                    b.op("scalar", ACT(z_sb[pr][:, :], bank[:, 0:S], AF.Silu), label=("sz", t), waits=w)
            b.dma("scalar", v_scr[t * 128:(t + 1) * 128, :], v_sb[pr], label=("vst", t),
                  waits=[("!", ("sv", t, 3))])

            if own and t % 4 == 3:
                PU = [P2, P3]
                for ec in range(16):
                    bank = PU[ec % 2]
                    for c in range(8):
                        w = []
                        if c == 0:
                            if ec <= 1:
                                w.append(("trfev", t, 6 + ec))
                            else:
                                w.append(("su", grp, ec - 2))
                            if ec == 0:
                                w.append(("trev8", t))
                        b.op("tensor", MM(bank[:, 0:512], Winp[:, c * F + ec * 128: c * F + (ec + 1) * 128],
                                          xn2nT[grp % 2][:, c * 512:(c + 1) * 512], c == 0, False), waits=w)
                    lab = [("ups", grp, ec)] + ([("xTfree", grp)] if ec == 15 else [])
                    b.op("tensor", MM(bank[:, 0:512], binrow_bf[0:1, ec * 128:(ec + 1) * 128],
                                      ones_bf[0:1, 0:512], False, True), label=lab)
                    w = [("ups", grp, ec)]
                    if ec >= 2:
                        w.append(("ust", grp, ec - 2))
                    elif grp >= 1:
                        w.append(("ust", grp - 1, 14 + ec))
                    b.op("scalar", ACT(u_sb[ec % 2][:, :], bank[:, 0:512], AF.Silu), label=("su", grp, ec),
                         waits=w)
                    b.dma("sync", uT_scr[ec * 128:(ec + 1) * 128, grp * 512:(grp + 1) * 512], u_sb[ec % 2],
                          label=("ust", grp, ec), waits=[("su", grp, ec)])
                b.dma("sync", xnT_scr.rearrange("(c p) t -> p c t", p=128)[:, :, grp * 512:(grp + 1) * 512],
                      xnT_b.rearrange("p (c t) -> p c t", c=8), label=("xTst", grp), waits=[("xnTb", t)])

            if own:
                po = (t % 2) * 8
                for c in range(8):
                    w = []
                    if c == 0:
                        w = [("trfev", t, 7), ("pre_gwT",)]
                        if t == 0:
                            w.append(("pre1",))
                        if t >= 2:
                            w.append(("g_ps", t - 2))
                    b.op("tensor", MM(PW[:, po:po + 8], xnTf[:, c * 128:(c + 1) * 128],
                                      gwT[:, c * 8:(c + 1) * 8], c == 0, False), waits=w)
                b.op("tensor", MM(PW[:, po:po + 8], ones_f[0:1, 0:128], gb[0:1, 0:8], False, True),
                     label=("gmm", t))
                lg = PW[:, po:po + 8]
                m1, m2, dd, w1v, w2v = st_[:, 0:1], st_[:, 1:2], st_[:, 2:3], st_[:, 3:4], st_[:, 4:5]
                eq1, msk, eq2, tmp = sq_scr[:, 0:8], sq_scr[:, 8:16], sq_scr[:, 16:24], sq_scr[:, 24:32]
                b.op("vector", (lambda e: e.reduce_max(m1, lg, axis=AX.X)), label=("gm1", t),
                     waits=[("gmm", t), ("sq2", t), ("ssq2v", t)] +
                           ([("dst", t)] if (debug and t < 2) else []))
                b.op("vector", TT(eq1, lg, m1.broadcast_to([128, 8]), ALU.is_equal),
                     label=("geq1", t), waits=[("!", ("gm1", t))])
                b.op("vector", STT(msk, eq1, -1e30, lg, ALU.mult, ALU.add),
                     label=("gmsk", t), waits=[("!", ("geq1", t))])
                b.op("vector", (lambda e: e.reduce_max(m2, msk, axis=AX.X)),
                     label=("gm2", t), waits=[("!", ("gmsk", t))])
                b.op("vector", TT(eq2, lg, m2.broadcast_to([128, 8]), ALU.is_equal), label=("g_ps", t),
                     waits=[("!", ("gm2", t))])
                b.op("vector", TT(dd, m2, m1, ALU.subtract), label=("gd", t))
                b.op("scalar", ACT(w1v, dd, AF.Sigmoid, scale=neg1_c[:, 0:1]), waits=[("gd", t)])
                b.op("scalar", ACT(w2v, dd, AF.Sigmoid), label=("gw", t))
                b.op("vector", TS(tmp, eq1, w1v, None, ALU.mult, ALU.bypass),
                     label=("gtmp", t), waits=[("gw", t)])
                b.op("vector", STT(W8[:, t * 8:(t + 1) * 8], eq2, w2v, tmp, ALU.mult, ALU.add),
                     label=("w8", t), waits=[("!", ("gtmp", t))])

        for t in range(NT):
            tile(t)
        tz = NT - 1
        ztranspose(tz)

        if debug:
            b.dma("sync", d_bin[:, :], binrow_bf[:, :], waits=[("kq", tz)])
            b.dma("sync", d_kT[:, :], kT[:, :], waits=[("kq", tz), ("w8", NOWN - 1)])
            b.dma("sync", d_qT[:, :], qT[:, :])
            b.dma("sync", d_w8[:, :], W8[:, :])

        P1LAST = [("trz", tz), ("kq", tz), ("sz", tz), ("vst", tz), ("ust", 7, 15), ("w8", NOWN - 1),
                  ("xTst", 7)]
        if stop_after < 2:
            b.wait("sync", *P1LAST)
            b.wait_count("sync", "QS")
            b.wait_count("sync", "QA")
            b.wait_count("sync", "QG")
            return

        b.dma("sync", Wout_sb.rearrange("p (c d) -> p c d", c=16),
              woutb.rearrange("(c p) d -> p c d", p=128), label=("lwout",), waits=[("wcout",)] + P1LAST)
        b.dma("scalar", boutr, boutb[:, :], label=("lbout",), waits=[("wcbout",)] + P1LAST)

        pairs = [(qb, s) for qb in range(NQB) for s in range(NSUP)]
        PB = [P2, P3, PW[:, 0:512], PW[:, 512:1024]]
        pidx = lambda qb, s: qb * NSUP + s

        def phaseA(qb, s):
            i = pidx(qb, s)
            PA = [P0, P1]
            for kk in range(KKS):
                kg = s * KKS + kk
                gi = i * KKS + kk
                w = []
                if gi >= 2:
                    w.append(("rl",) + divmod(gi - 2, KKS))
                if i == 0 and kk == 0:
                    w += P1LAST
                b.op("tensor", MM(PA[kk % 2][:, 0:512], kT[:, kg * 128:(kg + 1) * 128],
                                  qT[:, qb * 512:(qb + 1) * 512], True, True), label=("smm", i, kk), waits=w)
                w = [("smm", i, kk)]
                if gi >= 2:
                    w.append(("aTm",) + divmod(gi - 2, KKS))
                if i == 0 and kk == 0:
                    w += P1LAST
                b.op("scalar", ACT(rel_sb[kk % 2][:, :], PA[kk % 2][:, 0:512], AF.Relu), label=("rl", i, kk),
                     waits=w)
                w = [("rl", i, kk)]
                if kk == 0:
                    if i >= 2:
                        pq, ps_ = divmod(i - 2, NSUP)
                        w.append(("Wmm", pq, 3) if ps_ == NSUP - 1 else ("Bfree", i - 2))
                    if i == 0:
                        w += P1LAST
                b.op("vector", TT(aT[i % 2][:, kk * 512:(kk + 1) * 512], rel_sb[kk % 2], rel_sb[kk % 2],
                                  ALU.mult), label=("aTm", i, kk), waits=w)

        def phaseB(qb, s):
            i = pidx(qb, s)
            for p in range(4):
                for h in range(2):
                    j = (i * 4 + p) * 2 + h
                    w = []
                    if j >= 2:
                        w.append(("vread", j - 2))
                    if j == 0:
                        w += P1LAST
                    b.dma("sync", vbuf[j % 2].rearrange("p (c e) -> p c e", c=8),
                          v_scr.rearrange("(c p) e -> p c e", p=128)[:, s * KKS + h * 8: s * KKS + (h + 1) * 8,
                                                                    p * 512:(p + 1) * 512],
                          label=("lv", j), waits=w)
                for h in range(2):
                    j = (i * 4 + p) * 2 + h
                    for el in range(4):
                        for k8 in range(8):
                            w = []
                            if k8 == 0:
                                w.append(("lv", j))
                                if h == 0:
                                    if p >= 1:
                                        w.append(("fl", i, p - 1, el))
                                    elif i >= 1:
                                        w.append(("fl", i - 1, 3, el))
                                    if el == 0:
                                        w.append(("aTm", i, KKS - 1))
                                    if el == 2 and p == 0 and s == 0 and qb >= 1:
                                        w.append(("sev", qb - 1, 3))
                            lab = []
                            if el == 3 and k8 == 7:
                                lab.append(("vread", j))
                            if h == 1 and k8 == 7:
                                lab.append(("Bel", i, p, el))
                                if el == 3 and p == 3:
                                    lab.append(("Bfree", i))
                            b.op("tensor", MM(PB[el], vbuf[j % 2][:, k8 * 512 + el * 128: k8 * 512 + (el + 1) * 128],
                                              aT[i % 2][:, (h * 8 + k8) * 512:(h * 8 + k8 + 1) * 512],
                                              h == 0 and k8 == 0, h == 1 and k8 == 7),
                                 label=lab if lab else None, waits=w)
                for el in range(4):
                    ec = p * 4 + el
                    if s == 0:
                        b.op("vector", CP(acc[:, ec * 512:(ec + 1) * 512], PB[el]), label=("fl", i, p, el),
                             waits=[("Bel", i, p, el)])
                    else:
                        b.op("vector", TT(acc[:, ec * 512:(ec + 1) * 512], PB[el],
                                          acc[:, ec * 512:(ec + 1) * 512], ALU.add),
                             label=("fl", i, p, el), waits=[("Bel", i, p, el)])

        def blockend(qb):
            i7 = pidx(qb, NSUP - 1)
            mT = aT[i7 % 2]
            for ec in range(16):
                w = []
                if ec >= 2:
                    w.append(("mT", qb, ec - 2))
                elif qb >= 1:
                    w.append(("mT", qb - 1, 14 + ec))
                elif qb == 0 and ec <= 1:
                    w += P1LAST
                b.dma("scalar", u2_sb[ec % 2], uT_scr[ec * 128:(ec + 1) * 128, qb * 512:(qb + 1) * 512],
                      label=("lu", qb, ec), waits=w)
                b.op("vector", TT(mT[:, ec * 512:(ec + 1) * 512], u2_sb[ec % 2],
                                  acc[:, ec * 512:(ec + 1) * 512], ALU.mult), label=("mT", qb, ec),
                     waits=[("lu", qb, ec), ("fl", i7, 3, 3), ("Bfree", i7)])
            for tsub in range(4):
                tt = qb * 4 + tsub
                for ec in range(16):
                    w = [("mT", qb, ec)]
                    if ec == 0:
                        w.append(("lwout",))
                        if tsub >= 1:
                            w.append(("sev", qb, tsub - 1))
                        elif qb >= 1:
                            w.append(("sev", qb - 1, 3))
                    for half in range(2):
                        b.op("tensor", MM(PW[:, half * 512:(half + 1) * 512],
                                          mT[:, ec * 512 + tsub * 128: ec * 512 + tsub * 128 + 128],
                                          Wout_sb[:, ec * 1024 + half * 512: ec * 1024 + (half + 1) * 512],
                                          ec == 0, False), waits=w if half == 0 else [])
                for half in range(2):
                    b.op("tensor", MM(PW[:, half * 512:(half + 1) * 512], ones_bf[0:1, 0:128],
                                      boutr[0:1, half * 512:(half + 1) * 512], False, True),
                         label=("Wmm", qb, tsub) if half == 1 else None,
                         waits=[("lbout",)] if (qb == 0 and tsub == 0 and half == 0) else [])
                w = []
                if tt >= 2:
                    w.append(("sev", (tt - 2) // 4, (tt - 2) % 4))
                elif qb == 0:
                    w += P1LAST
                b.dma("scalar", xn2_sb[tsub % 2], xn_scr[tt * 128:(tt + 1) * 128, :], label=("lxn", qb, tsub),
                      waits=w)
                w = [("Wmm", qb, tsub), ("lxn", qb, tsub)]
                if tt >= 2:
                    w.append(("sst", (tt - 2) // 4, (tt - 2) % 4))
                b.op("vector", TT(s_sb[tsub % 2], PW[:, 0:1024], xn2_sb[tsub % 2], ALU.add),
                     label=("sev", qb, tsub), waits=w)
                b.dma("scalar", s_scr[tt * 128:(tt + 1) * 128, :], s_sb[tsub % 2], label=("sst", qb, tsub),
                      waits=[("sev", qb, tsub)])

        phaseA(*pairs[0])
        for n, (qb, s) in enumerate(pairs):
            if n + 1 < len(pairs):
                phaseA(*pairs[n + 1])
            phaseB(qb, s)
            if s == NSUP - 1:
                blockend(qb)

        P2LAST = [("Wmm", NQB - 1, 3), ("sev", NQB - 1, 3), ("sst", NQB - 1, 3),
                  ("rl", len(pairs) - 1, KKS - 1), ("lv", len(pairs) * 8 - 1)]
        if stop_after < 3:
            b.wait("sync", *P2LAST)
            b.wait_count("sync", "QS")
            b.wait_count("sync", "QA")
            b.wait_count("sync", "QG")
            return

        PS_PAIR = [(PW[:, 0:512], PW[:, 512:1024]), (P2, P3)]

        def unit(q, e_, hh):
            u = q * 16 + e_ * 2 + hh
            par = u % 2
            w = [("wc", e_, hh)]
            if u >= 2:
                w.append(("uend", u - 2))
            if u == 0:
                w += P2LAST
            b.dma("sync", w0s[par].rearrange("p (c h) -> p c h", c=8),
                  w0b.rearrange("e (c p) h -> e p c h", p=128)[e_][:, :, hh * 1024:(hh + 1) * 1024],
                  label=("lw0", u), waits=w)
            b.dma("sync", w1s[par].rearrange("p (c h) -> p c h", c=8),
                  w1b.rearrange("e (c p) h -> e p c h", p=128)[e_][:, :, hh * 1024:(hh + 1) * 1024],
                  label=("lw1", u))
            b.dma("sync", w2s[par].rearrange("p (c d) -> p c d", c=8),
                  w2b.rearrange("e (c p) d -> e p c d", p=128)[e_][:, hh * 8:(hh + 1) * 8, :],
                  label=("lw2", u))
            w = [("wcb2",)] if u < 2 else [("uend", u - 2)]
            if u == 0:
                w += P2LAST
            b.dma("gpsimd", b0c[par], b0_h[e_, :, hh * 8:(hh + 1) * 8], label=("lb0", u), waits=w)
            b.dma("gpsimd", b1c[par], b1_h[e_, :, hh * 8:(hh + 1) * 8], label=("lb1", u))
            if hh == 0:
                b.dma("gpsimd", b2r[e_ % 2], b2b[e_:e_ + 1, :], label=("lb2", q, e_))
            for tg in range(2):
                tgi = u * 2 + tg
                for hc in range(8):
                    for c in range(8):
                        w = []
                        if c == 0:
                            if hc >= 1:
                                w.append(("h0ev", u, tg, hc - 1))
                            elif tgi >= 1:
                                w.append(("h0ev", (tgi - 1) // 2, (tgi - 1) % 2, 7))
                            if hc == 0:
                                w += [("lw0", u), ("lw1", u), ("lxq", q)]
                                if u == 0 and tg == 0:
                                    w += P2LAST
                        b.op("tensor", MM(P0[:, 0:512],
                                          w0s[par][:, c * 1024 + hc * 128: c * 1024 + (hc + 1) * 128],
                                          xnT_q[:, c * 1024 + tg * 512: c * 1024 + tg * 512 + 512],
                                          c == 0, c == 7),
                             label=("h0", u, tg, hc) if c == 7 else None, waits=w)
                    for c in range(8):
                        w = []
                        if c == 0:
                            if hc >= 1:
                                w.append(("h1ev", u, tg, hc - 1))
                            elif tgi >= 1:
                                w.append(("h1ev", (tgi - 1) // 2, (tgi - 1) % 2, 7))
                        b.op("tensor", MM(P1[:, 0:512],
                                          w1s[par][:, c * 1024 + hc * 128: c * 1024 + (hc + 1) * 128],
                                          xnT_q[:, c * 1024 + tg * 512: c * 1024 + tg * 512 + 512],
                                          c == 0, c == 7),
                             label=("h1", u, tg, hc) if c == 7 else None, waits=w)
                    wv = [("h0", u, tg, hc), ("lb0", u)]
                    if hc >= 2:
                        wv.append(("act", u, tg, hc - 2))
                    elif tgi >= 1:
                        wv.append(("act", (tgi - 1) // 2, (tgi - 1) % 2, 6 + hc))
                    b.op("scalar", ACT(h0_sb[hc % 2][:, :], P0[:, 0:512], AF.Identity, bias=b0c[par][:, hc:hc + 1]),
                         label=("h0ev", u, tg, hc), waits=wv)
                    b.op("scalar", ACT(si_sb[hc % 2][:, :], P1[:, 0:512], AF.Silu, bias=b1c[par][:, hc:hc + 1]),
                         label=("h1ev", u, tg, hc), waits=[("h1", u, tg, hc), ("lb1", u)])
                    wv = [("h1ev", u, tg, hc)]
                    if hc == 0 and tgi >= 1:
                        wv.append(("w2", (tgi - 1) // 2, (tgi - 1) % 2, 3))
                    b.op("vector", TT(actT[:, hc * 512:(hc + 1) * 512], h0_sb[hc % 2], si_sb[hc % 2], ALU.mult),
                         label=("act", u, tg, hc), waits=wv)
                for tsub in range(4):
                    pp = PS_PAIR[tsub % 2]
                    ti = tgi * 4 + tsub
                    w = [("act", u, tg, 7), ("lw2", u)]
                    if ti >= 2:
                        w.append(("ae", (ti - 2) // 8, ((ti - 2) // 4) % 2, (ti - 2) % 4))
                    if u == 0 and tg == 0 and tsub <= 1:
                        w += P2LAST
                    last_is_mm = (hh == 1)
                    for hc in range(8):
                        for half in range(2):
                            lab = None
                            if last_is_mm and hc == 7 and half == 1:
                                lab = [("w2", u, tg, tsub)] + ([("uend", u)] if (tg == 1 and tsub == 3) else [])
                            b.op("tensor", MM(pp[half],
                                              actT[:, hc * 512 + tsub * 128: hc * 512 + tsub * 128 + 128],
                                              w2s[par][:, hc * 1024 + half * 512: hc * 1024 + (half + 1) * 512],
                                              hc == 0, hh == 1 and hc == 7),
                                 label=lab, waits=w if (hc == 0 and half == 0) else [])
                    if hh == 0:
                        for half in range(2):
                            lab = None
                            if half == 1:
                                lab = [("w2", u, tg, tsub)] + ([("uend", u)] if (tg == 1 and tsub == 3) else [])
                            b.op("tensor", MM(pp[half], ones_bf[0:1, 0:128],
                                              b2r[e_ % 2][0:1, half * 512:(half + 1) * 512], False, True),
                                 label=lab, waits=[("lb2", q, e_)] if half == 0 else [])
                    tt = q * 8 + tg * 4 + tsub
                    wcol = W8[:, tt * 8 + e_: tt * 8 + e_ + 1]
                    dst = acc_q[:, (tg * 4 + tsub) * 1024:(tg * 4 + tsub + 1) * 1024]
                    first = (e_ == 0 and hh == 0)
                    for half in range(2):
                        d2 = dst[:, half * 512:(half + 1) * 512]
                        lab = ("ae", u, tg, tsub) if half == 1 else None
                        if first:
                            b.op("vector", TS(d2, pp[half], wcol, None, ALU.mult, ALU.bypass), label=lab,
                                 waits=[("w2", u, tg, tsub)] if half == 0 else [])
                        else:
                            b.op("vector", STT(d2, pp[half], wcol, d2, ALU.mult, ALU.add), label=lab,
                                 waits=[("w2", u, tg, tsub)] if half == 0 else [])

        def quarter(q):
            w = [("uend", q * 16 - 1)] if q >= 1 else list(P2LAST)
            b.dma("sync", xnT_q.rearrange("p (c t) -> p c t", c=8),
                  xnT_scr.rearrange("(c p) t -> p c t", p=128)[:, :, q * 1024:(q + 1) * 1024],
                  label=("lxq", q), waits=w)
            for e_ in range(FE):
                for hh in range(2):
                    unit(q, e_, hh)
            ul = q * 16 + 15
            for ts8 in range(8):
                par = ts8 % 2
                tt = q * 8 + ts8
                w = [("ae", ul, ts8 // 4, ts8 % 4)]
                if ts8 >= 2:
                    w.append(("oc", q, ts8 - 2))
                elif q >= 1:
                    w.append(("oc", q - 1, 6 + ts8))
                b.op("scalar", ACT(g_sb[par][:, :], acc_q[:, ts8 * 1024:(ts8 + 1) * 1024], AF.Silu),
                     label=("g", q, ts8), waits=w)
                w = [("sg", q, ts8 - 2)] if ts8 >= 2 else ([("sg", q - 1, 6 + ts8)] if q >= 1 else [("Wmm", NQB - 1, 3)])
                b.dma("scalar", s3_sb[par], s_scr[tt * 128:(tt + 1) * 128, :], label=("ls3", q, ts8), waits=w)
                w = [("oc", q, ts8 - 2)] if ts8 >= 2 else ([("oc", q - 1, 6 + ts8)] if q >= 1 else [("Wmm", NQB - 1, 3)])
                b.dma("sync", x3_sb[par], xb[tt * 128:(tt + 1) * 128, :], label=("lx3", q, ts8), waits=w)
                b.op("vector", TT(s3_sb[par], s3_sb[par], g_sb[par], ALU.mult), label=("sg", q, ts8),
                     waits=[("g", q, ts8), ("ls3", q, ts8)])
                w = [("lx3", q, ts8), ("sg", q, ts8)]
                if ts8 >= 2:
                    w.append(("ost", q, ts8 - 2))
                elif q >= 1:
                    w.append(("ost", q - 1, 6 + ts8))
                b.op("vector", TT(x3_sb[par], s3_sb[par], x3_sb[par], ALU.add), label=("oc", q, ts8),
                     waits=w + [("!", ("sg", q, ts8))])
                b.dma("sync", O[tt * 128:(tt + 1) * 128, :], x3_sb[par], label=("ost", q, ts8),
                      waits=[("oc", q, ts8)])

        for q in range(4):
            quarter(q)

        b.wait("sync", ("ost", 3, 7))
        b.wait_count("sync", "QS")
        b.wait_count("sync", "QA")
        b.wait_count("sync", "QG")

    b.start_pass(COUNT)
    prog()
    b.final = dict(b.cnt)

    with nc.Block() as blk:
        @blk.sync
        def _(eng):
            b.start_pass("sync", eng)
            prog()

        @blk.gpsimd
        def _(eng):
            b.start_pass("gpsimd", eng)
            prog()

        @blk.tensor
        def _(eng):
            b.start_pass("tensor", eng)
            prog()

        @blk.scalar
        def _(eng):
            b.start_pass("scalar", eng)
            prog()

        @blk.vector
        def _(eng):
            b.start_pass("vector", eng)
            prog()

    return nc


def _prep_inputs(inp):
    f32 = lambda a: np.ascontiguousarray(np.asarray(a, dtype=np.float32))
    X = f32(inp["L0_attn_x"]).reshape(T, D)
    w0 = f32(inp["L0_ffn_w0"])
    w1 = f32(inp["L0_ffn_w1"])
    w2 = f32(inp["L0_ffn_w2"])
    com = {
        "w_in": f32(inp["L0_attn_W_in"]),
        "b_in": f32(inp["L0_attn_b_in"]).reshape(1, F),
        "lng": np.ascontiguousarray(f32(inp["L0_attn_ln_g"]).reshape(8, 128).T),
        "lnb": np.ascontiguousarray(f32(inp["L0_attn_ln_b"]).reshape(8, 128).T),
        "gq": f32(inp["L0_attn_gamma_q"]).reshape(128, 1),
        "bq": f32(inp["L0_attn_beta_q"]).reshape(128, 1),
        "gk": f32(inp["L0_attn_gamma_k"]).reshape(128, 1),
        "bk": f32(inp["L0_attn_beta_k"]).reshape(128, 1),
        "wout": f32(inp["L0_attn_W_out"]),
        "bout": f32(inp["L0_attn_b_out"]).reshape(1, D),
        "gwt": np.ascontiguousarray(f32(inp["L0_ffn_gate_w"]).T),
        "gb": f32(inp["L0_ffn_gate_b"]).reshape(1, FE),
        "w0t": np.ascontiguousarray(w0.transpose(0, 2, 1)),
        "w1t": np.ascontiguousarray(w1.transpose(0, 2, 1)),
        "w2t": np.ascontiguousarray(w2.transpose(0, 2, 1)),
        "b0": np.ascontiguousarray(f32(inp["L0_ffn_b0"]).reshape(FE, 16, 128).transpose(0, 2, 1)),
        "b1": np.ascontiguousarray(f32(inp["L0_ffn_b1"]).reshape(FE, 16, 128).transpose(0, 2, 1)),
        "b2": f32(inp["L0_ffn_b2"]),
    }
    in_maps = []
    for c in range(NC):
        bb = c // 4
        off = (c % 4) * TQ
        batch = X[bb * L:(bb + 1) * L]
        xbc = np.ascontiguousarray(
            np.concatenate([batch[off:off + TQ], batch[:off], batch[off + TQ:]], axis=0))
        m = dict(com)
        m["xb"] = xbc
        in_maps.append(m)
    return in_maps


def kernel(**inp):
    in_maps = _prep_inputs(inp)
    nc = build(stop_after=3, debug=DBG)
    res = run_bass_kernel_spmd(nc, in_maps, core_ids=list(range(NC)))
    out = np.concatenate([np.asarray(r["o"], dtype=np.float32) for r in res.results], axis=0)
    return out.reshape(NB, L, D)


# version 03 (final)
# speedup vs baseline: 1.0406x; 1.0406x over previous
import numpy as np
import ml_dtypes
import concourse.bass as bass
import concourse.mybir as mybir
from concourse.bass_utils import run_bass_kernel_spmd

F32 = mybir.dt.float32
BF16 = mybir.dt.bfloat16
AF = mybir.ActivationFunctionType
ALU = mybir.AluOpType
AX = mybir.AxisListType

D = 1024
L = 16384
NB = 2
T = NB * L
NC = 8
TQ = T // NC
E = 2048
S = 128
FE, H = 8, 2048
F = 2 * E + S
EPS = 1e-5
INVL = 1.0 / L
NT = L // 128
NOWN = TQ // 128

QB = 512
NQB = TQ // QB
SUP = 2048
NSUP = L // SUP
KKS = SUP // 128

DBG = False

COUNT = "count"
QSEM = {"sync": "QS", "scalar": "QA", "gpsimd": "QG"}
ESEM = {"tensor": "TE", "vector": "VE", "scalar": "SE", "gpsimd": "GE", "sync": "SY"}


class Bld:
    def __init__(self, nc):
        self.nc = nc
        self.sems = {n: nc.alloc_semaphore(n) for n in ("QS", "QA", "QG", "TE", "VE", "SE", "GE", "SY")}
        self.labels = {}
        self.cnt = {k: 0 for k in self.sems}
        self.final = None
        self.mode = COUNT
        self.eng = None
        self.engname = None

    def start_pass(self, mode, eng=None):
        self.mode = mode
        self.eng = eng
        self.engname = None if mode == COUNT else mode

    def _wait1(self, label):
        force = False
        if isinstance(label, tuple) and len(label) == 2 and label[0] == "!":
            force = True
            label = label[1]
        s, v, peng = self.labels[label]
        if peng == self.engname and not force:
            return
        self.eng.wait_ge(self.sems[s], v)

    def _lab(self, label, semname, q):
        for lb in (label if isinstance(label, list) else [label]):
            assert lb not in self.labels, f"duplicate label {lb}"
            self.labels[lb] = (semname, self.cnt[semname], q)

    def dma(self, q, out, in_, label=None, waits=(), **kw):
        sn = QSEM[q]
        if self.mode == COUNT:
            self.cnt[sn] += 16
            if label is not None:
                self._lab(label, sn, q)
        elif self.engname == q:
            for w in waits:
                self._wait1(w)
            self.eng.dma_start(out=out, in_=in_, **kw).then_inc(self.sems[sn], 16)

    def op(self, engname, emit, label=None, waits=()):
        sn = ESEM[engname]
        if self.mode == COUNT:
            if label is not None:
                self.cnt[sn] += 1
                self._lab(label, sn, engname)
        elif self.engname == engname:
            for w in waits:
                self._wait1(w)
            i = emit(self.eng)
            if label is not None:
                i.then_inc(self.sems[sn], 1)

    def wait(self, engname, *labels):
        if self.mode == engname:
            for w in labels:
                self._wait1(w)

    def wait_count(self, engname, semname):
        if self.mode == engname:
            self.eng.wait_ge(self.sems[semname], self.final[semname])


def build(stop_after=3, debug=False):
    nc = bass.Bass("TRN2", target_bir_lowering=False, debug=False)
    ap = lambda h: h.ap()

    xb = ap(nc.dram_tensor("xb", [L, D], F32, kind="ExternalInput"))
    W_in = ap(nc.dram_tensor("w_in", [D, F], F32, kind="ExternalInput"))
    b_in = ap(nc.dram_tensor("b_in", [1, F], F32, kind="ExternalInput"))
    lng_h = ap(nc.dram_tensor("lng", [128, 8], F32, kind="ExternalInput"))
    lnb_h = ap(nc.dram_tensor("lnb", [128, 8], F32, kind="ExternalInput"))
    gq_h = ap(nc.dram_tensor("gq", [128, 1], F32, kind="ExternalInput"))
    bq_h = ap(nc.dram_tensor("bq", [128, 1], F32, kind="ExternalInput"))
    gk_h = ap(nc.dram_tensor("gk", [128, 1], F32, kind="ExternalInput"))
    bk_h = ap(nc.dram_tensor("bk", [128, 1], F32, kind="ExternalInput"))
    wout_f = ap(nc.dram_tensor("wout", [E, D], F32, kind="ExternalInput"))
    bout_f = ap(nc.dram_tensor("bout", [1, D], F32, kind="ExternalInput"))
    gwT_h = ap(nc.dram_tensor("gwt", [D, FE], F32, kind="ExternalInput"))
    gb_h = ap(nc.dram_tensor("gb", [1, FE], F32, kind="ExternalInput"))
    w0T_h = ap(nc.dram_tensor("w0t", [FE, D, H], F32, kind="ExternalInput"))
    w1T_h = ap(nc.dram_tensor("w1t", [FE, D, H], F32, kind="ExternalInput"))
    w2T_h = ap(nc.dram_tensor("w2t", [FE, H, D], F32, kind="ExternalInput"))
    b0_h = ap(nc.dram_tensor("b0", [FE, 128, 16], F32, kind="ExternalInput"))
    b1_h = ap(nc.dram_tensor("b1", [FE, 128, 16], F32, kind="ExternalInput"))
    b2_h = ap(nc.dram_tensor("b2", [FE, D], F32, kind="ExternalInput"))

    O = ap(nc.dram_tensor("o", [TQ, D], F32, kind="ExternalOutput"))

    sk = "ExternalOutput" if debug else "Internal"
    v_scr = ap(nc.dram_tensor("v_scr", [L, E], BF16, kind=sk))
    uT_scr = ap(nc.dram_tensor("uT_scr", [E, TQ], F32, kind=sk))
    xnT_scr = ap(nc.dram_tensor("xnT_scr", [D, TQ], BF16, kind=sk))
    xn_scr = ap(nc.dram_tensor("xn_scr", [TQ, D], F32, kind=sk))
    s_scr = ap(nc.dram_tensor("s_scr", [TQ, D], F32, kind=sk))
    w0b = ap(nc.dram_tensor("w0b", [FE, D, H], BF16, kind="Internal"))
    w1b = ap(nc.dram_tensor("w1b", [FE, D, H], BF16, kind="Internal"))
    w2b = ap(nc.dram_tensor("w2b", [FE, H, D], BF16, kind="Internal"))
    woutb = ap(nc.dram_tensor("woutb", [E, D], BF16, kind="Internal"))
    boutb = ap(nc.dram_tensor("boutb", [1, D], BF16, kind="Internal"))
    b2b = ap(nc.dram_tensor("b2b", [FE, D], BF16, kind="Internal"))

    identb_h = ap(nc.inline_tensor(np.eye(128, dtype=ml_dtypes.bfloat16), name="identb_c"))
    identf_h = ap(nc.inline_tensor(np.eye(128, dtype=np.float32), name="identf_c"))

    if debug:
        d_kT = ap(nc.dram_tensor("d_kT", [S, L], BF16, kind="ExternalOutput"))
        d_qT = ap(nc.dram_tensor("d_qT", [S, TQ], BF16, kind="ExternalOutput"))
        d_w8 = ap(nc.dram_tensor("d_w8", [128, NOWN * 8], F32, kind="ExternalOutput"))
        d_bin = ap(nc.dram_tensor("d_bin", [1, F], BF16, kind="ExternalOutput"))

    base = (nc.sbuf_base + 31) & ~31
    top = nc.sbuf_top

    class Alloc:
        def __init__(self, start):
            self.off = start

        def t(self, name, shape, dt):
            sz = int(np.prod(shape[1:])) * (2 if dt == BF16 else 4)
            o = (self.off + 31) & ~31
            h = nc.alloc_sbuf_tensor_at(name, shape, dt, offset=o)
            self.off = o + sz
            assert self.off <= top, f"SBUF overflow at {name}: {(self.off - base)} bytes"
            return h.ap()

    pa = Alloc(base)
    identb = pa.t("identb", [128, 128], BF16)
    identf = pa.t("identf", [128, 128], F32)
    ones_bf = pa.t("ones_bf", [1, 512], BF16)
    ones_f = pa.t("ones_f", [1, 128], F32)
    gq_c = pa.t("gq_c", [128, 1], F32)
    bq_c = pa.t("bq_c", [128, 1], F32)
    gk_c = pa.t("gk_c", [128, 1], F32)
    bk_c = pa.t("bk_c", [128, 1], F32)
    gqp = pa.t("gqp", [128, 1], F32)
    bqp = pa.t("bqp", [128, 1], F32)
    lng = pa.t("lng_s", [128, 8], F32)
    lnb = pa.t("lnb_s", [128, 8], F32)
    gwT = pa.t("gwT_s", [128, 64], F32)
    gb = pa.t("gb_s", [1, 8], F32)
    W8 = pa.t("W8", [128, NOWN * 8], F32)
    eps_c = pa.t("eps_c", [128, 1], F32)
    neg1_c = pa.t("neg1_c", [128, 1], F32)
    pers_end = pa.off

    aa = Alloc(pers_end)
    kT = aa.t("kT", [128, L], BF16)
    qT = aa.t("qT", [128, TQ], BF16)
    regA_end = aa.off

    p1 = Alloc(regA_end)
    Winp = p1.t("Winp", [128, 8 * F], BF16)
    binrow_bf = p1.t("binrow_bf", [1, F], BF16)
    stat = [p1.t(f"stat{i}", [128, 16], F32) for i in range(2)]
    zT_sb = p1.t("zT_sb", [128, 128], BF16)
    z_sb = [p1.t(f"z_sb{i}", [128, 128], BF16) for i in range(2)]
    ovl = p1.off
    x_sb = [p1.t(f"x_sb{i}", [128, D], F32) for i in range(3)]
    xn_sb = [p1.t(f"xn_sb{i}", [128, D], F32) for i in range(2)]
    xn2n_sb = [p1.t(f"xn2n_sb{i}", [128, D], BF16) for i in range(2)]
    sq_scr = p1.t("sq_scr", [128, D], F32)
    xnTf = p1.t("xnTf", [128, D], F32)
    xn2nT = [p1.t(f"xn2nT{i}", [128, 8 * 512], BF16) for i in range(2)]
    xnT_b = p1.t("xnT_b", [128, 8 * 512], BF16)
    v_sb = [p1.t(f"v_sb{i}", [128, E], BF16) for i in range(2)]
    u_sb = [p1.t(f"u_sb{i}", [128, 512], F32) for i in range(2)]

    p2 = Alloc(regA_end)
    aT = [p2.t(f"aT{i}", [128, KKS * 512], BF16) for i in range(2)]
    acc = p2.t("acc2", [128, 16 * 512], F32)
    Wout_sb = p2.t("Wout_sb", [128, 16 * D], BF16)
    vbuf = [p2.t(f"vbuf{i}", [128, 8 * 512], BF16) for i in range(2)]
    rel_sb = [p2.t(f"rel_sb{i}", [128, 512], F32) for i in range(2)]
    u2_sb = [p2.t(f"u2_sb{i}", [128, 512], F32) for i in range(2)]
    xn2_sb = [p2.t(f"xn2_sb{i}", [128, D], F32) for i in range(2)]
    s_sb = [p2.t(f"s_sb{i}", [128, D], F32) for i in range(2)]
    boutr = p2.t("boutr", [1, D], BF16)

    p3 = Alloc(pers_end)
    w0s = [p3.t(f"w0s{i}", [128, 8 * 1024], BF16) for i in range(2)]
    w1s = [p3.t(f"w1s{i}", [128, 8 * 1024], BF16) for i in range(2)]
    w2s = [p3.t(f"w2s{i}", [128, 8 * 1024], BF16) for i in range(2)]
    xnT_q = p3.t("xnT_q", [128, 8 * 1024], BF16)
    acc_q = p3.t("acc_q", [128, 8 * 1024], F32)
    actT = p3.t("actT", [128, 8 * 512], BF16)
    h0_sb = [p3.t(f"h0_sb{i}", [128, 512], F32) for i in range(2)]
    si_sb = [p3.t(f"si_sb{i}", [128, 512], F32) for i in range(2)]
    b0c = [p3.t(f"b0c{i}", [128, 8], F32) for i in range(2)]
    b1c = [p3.t(f"b1c{i}", [128, 8], F32) for i in range(2)]
    b2r = [p3.t(f"b2r{i}", [1, D], BF16) for i in range(2)]
    g_sb = [p3.t(f"g_sb{i}", [128, D], F32) for i in range(2)]
    s3_sb = [p3.t(f"s3_sb{i}", [128, D], F32) for i in range(2)]
    x3_sb = [p3.t(f"x3_sb{i}", [128, D], F32) for i in range(2)]

    P0 = nc.alloc_psum_tensor("P0", [128, 512], F32).ap()
    P1 = nc.alloc_psum_tensor("P1", [128, 512], F32).ap()
    P2 = nc.alloc_psum_tensor("P2", [128, 512], F32).ap()
    P3 = nc.alloc_psum_tensor("P3", [128, 512], F32).ap()
    PW = nc.alloc_psum_tensor("PWI", [128, 1024], F32).ap()
    PT0 = nc.alloc_psum_tensor("PT0", [128, 1024], BF16).ap()
    PT1 = nc.alloc_psum_tensor("PT1", [128, 1024], BF16).ap()
    PT = [PT0, PT1]

    b = Bld(nc)

    def prog():
        MM = lambda out, l, r, st, sp: (lambda e: e.matmul(out, l, r, start=st, stop=sp, skip_group_check=True))
        TS = lambda out, i0, s1, s2, o0, o1: (lambda e: e.tensor_scalar(out=out, in0=i0, scalar1=s1, scalar2=s2, op0=o0, op1=o1))
        TT = lambda out, i0, i1, o: (lambda e: e.tensor_tensor(out=out, in0=i0, in1=i1, op=o))
        STT = lambda out, i0, sc, i1, o0, o1: (lambda e: e.scalar_tensor_tensor(out=out, in0=i0, scalar=sc, in1=i1, op0=o0, op1=o1))
        ACT = lambda out, i, f, bias=0.0, scale=1.0, acc=None: (lambda e: e.activation(out=out, in_=i, func=f, bias=bias, scale=scale, accum_out=acc))
        CP = lambda out, i: (lambda e: e.tensor_copy(out, i))

        b.dma("sync", identb, identb_h[:, :])
        b.dma("sync", identf, identf_h[:, :])
        b.dma("sync", gq_c, gq_h[:, :])
        b.dma("sync", bq_c, bq_h[:, :])
        b.dma("sync", gk_c, gk_h[:, :])
        b.dma("sync", bk_c, bk_h[:, :])
        b.dma("sync", lng, lng_h[:, :])
        b.dma("sync", lnb, lnb_h[:, :])
        b.dma("sync", gwT.rearrange("p (c e) -> p c e", c=8),
              gwT_h.rearrange("(c p) e -> p c e", p=128), label=("pre_gwT",))
        b.dma("sync", gb, gb_h[:, :], label=("pre1",))
        b.dma("gpsimd", woutb[:, :], wout_f[:, :], label=("wcout",))
        b.dma("gpsimd", boutb[:, :], bout_f[:, :], label=("wcbout",))
        b.dma("gpsimd", b2b[:, :], b2_h[:, :], label=("wcb2",))
        b.dma("gpsimd", Winp.rearrange("p (c f) -> p c f", c=8),
              W_in.rearrange("(c p) f -> p c f", p=128), label=("lwinp",))
        b.dma("gpsimd", binrow_bf[:, :], b_in[:, :], label=("lbin",))
        for e_ in range(FE):
            for hh in range(2):
                b.dma("gpsimd", w0b[e_, :, hh * 1024:(hh + 1) * 1024], w0T_h[e_, :, hh * 1024:(hh + 1) * 1024])
                b.dma("gpsimd", w1b[e_, :, hh * 1024:(hh + 1) * 1024], w1T_h[e_, :, hh * 1024:(hh + 1) * 1024])
                b.dma("gpsimd", w2b[e_, hh * 1024:(hh + 1) * 1024, :], w2T_h[e_, hh * 1024:(hh + 1) * 1024, :],
                      label=("wc", e_, hh))
        b.op("vector", lambda e: e.memset(ones_bf[:, :], 1.0))
        b.op("vector", lambda e: e.memset(ones_f[:, :], 1.0))
        b.op("vector", lambda e: e.memset(eps_c[:, :], EPS))
        b.op("vector", lambda e: e.memset(neg1_c[:, :], -1.0))
        b.op("vector", TS(gqp, gq_c, INVL, None, ALU.mult, ALU.bypass), waits=[("pre1",)])
        b.op("vector", TS(bqp, bq_c, INVL, None, ALU.mult, ALU.bypass), label=("qscale",))

        def ztranspose(tp):
            b.op("tensor", (lambda e: e.transpose(PT[tp % 2][:, 128:256], z_sb[tp % 2], identb)),
                 label=("trz", tp),
                 waits=[("sz", tp), ("trev", tp, 7)] + ([("zev", tp - 2)] if tp >= 2 else []))
            b.op("vector", CP(zT_sb[:, :], PT[tp % 2][:, 128:256]), label=("zev", tp), waits=[("trz", tp)])
            pw_ = [("pre1",)] if tp == 0 else []
            if tp < NOWN:
                b.op("vector", TS(qT[:, tp * 128:(tp + 1) * 128], zT_sb, gqp[:, 0:1], bqp[:, 0:1],
                                  ALU.mult, ALU.add), waits=[("qscale",), ("!", ("zev", tp))] + pw_)
            b.op("vector", TS(kT[:, tp * 128:(tp + 1) * 128], zT_sb, gk_c[:, 0:1], bk_c[:, 0:1],
                              ALU.mult, ALU.add), label=("kq", tp), waits=[("!", ("zev", tp))] + pw_)

        def tile(t):
            own = t < NOWN
            grp = t // 4
            pr = t % 2
            st_ = stat[pr]
            b.dma("sync", x_sb[t % 3], xb[t * 128:(t + 1) * 128, :], label=("lx", t),
                  waits=[("xfree", t - 3), ("r1", t - 3)] if t >= 3 else [])
            sw = [("lx", t)]
            if t >= 2:
                sw.append(("xn2n", t - 2))
            if 0 <= t - 1 < NOWN:
                sw.append(("w8", t - 1))
            if t >= 1:
                sw.append(("ssq1v", t - 1))
            b.op("scalar", ACT(sq_scr[:, :], x_sb[t % 3], AF.Square),
                 label=("sq1", t), waits=sw)
            b.op("vector", (lambda e: e.reduce_sum(st_[:, 0:1], x_sb[t % 3], axis=AX.X)),
                 label=("s1v", t), waits=[("lx", t), ("sq1", t)])
            b.op("vector", (lambda e: e.reduce_sum(st_[:, 1:2], sq_scr[:, :], axis=AX.X)),
                 label=("ssq1v", t))
            b.op("vector", TS(st_[:, 2:3], st_[:, 0:1], 1.0 / D, None, ALU.mult, ALU.bypass),
                 label=("m1v", t), waits=[("!", ("s1v", t))])
            b.op("vector", TT(st_[:, 3:4], st_[:, 2:3], st_[:, 2:3], ALU.mult),
                 label=("m1sv", t), waits=[("!", ("m1v", t))])
            b.op("vector", STT(st_[:, 4:5], st_[:, 1:2], 1.0 / D, st_[:, 3:4], ALU.mult, ALU.subtract),
                 label=("v1", t), waits=[("!", ("m1sv", t)), ("!", ("ssq1v", t))])
            b.op("scalar", ACT(st_[:, 5:6], st_[:, 4:5], AF.Sqrt, bias=eps_c[:, 0:1]), label=("sd1", t), waits=[("v1", t)])
            b.op("vector", (lambda e: e.reciprocal(st_[:, 6:7], st_[:, 5:6])), label=("r1v", t), waits=[("sd1", t)])
            b.op("vector", STT(st_[:, 7:8], st_[:, 2:3], -1.0, st_[:, 6:7], ALU.mult, ALU.mult), label=("r1", t),
                 waits=[("!", ("r1v", t))])
            b.op("vector", TT(st_[:, 8:9], st_[:, 6:7], st_[:, 6:7], ALU.mult),
                 label=("rsq", t), waits=[("!", ("r1", t))])
            b.op("vector", TT(st_[:, 9:10], st_[:, 4:5], st_[:, 8:9], ALU.mult),
                 label=("v2", t), waits=[("!", ("rsq", t))])
            b.op("scalar", ACT(st_[:, 10:11], st_[:, 9:10], AF.Sqrt, bias=eps_c[:, 0:1]),
                 label=("sd2", t), waits=[("v2", t)])
            b.op("vector", (lambda e: e.reciprocal(st_[:, 11:12], st_[:, 10:11])),
                 label=("r2v", t), waits=[("sd2", t)])
            b.op("vector", TT(st_[:, 12:13], st_[:, 6:7], st_[:, 11:12], ALU.mult),
                 label=("Rv", t), waits=[("!", ("r2v", t))])
            b.op("vector", STT(st_[:, 13:14], st_[:, 2:3], -1.0, st_[:, 12:13], ALU.mult, ALU.mult),
                 label=("nb2v", t), waits=[("!", ("Rv", t))])
            b.op("scalar", ACT(xn2n_sb[pr][:, :], x_sb[t % 3], AF.Identity, bias=st_[:, 13:14],
                               scale=st_[:, 12:13]),
                 label=[("xn2n", t)] + ([] if own else [("xfree", t)]),
                 waits=[("nb2v", t)] + ([("tr", t - 2, 7)] if t >= 2 else []))
            if own:
                xnw = [("r1", t)]
                if 0 <= t - 2 < NOWN:
                    xnw += [("trf", t - 2, 7), ("xnst", t - 2)]
                b.op("scalar", ACT(xn_sb[pr][:, :], x_sb[t % 3], AF.Identity, bias=st_[:, 7:8],
                                   scale=st_[:, 6:7]),
                     label=[("xn", t), ("xfree", t)], waits=xnw)
                b.dma("sync", xn_scr[t * 128:(t + 1) * 128, :], xn_sb[pr], label=("xnst", t), waits=[("xn", t)])

            if t >= 1:
                ztranspose(t - 1)

            for c in range(8):
                ci = t * 8 + c
                w = [("xn2n", t)]
                if ci >= 2:
                    w.append(("trev",) + divmod(ci - 2, 8))
                if c == 0 and t >= 1:
                    w.append(("zev", t - 1))
                if c == 0 and t == 0:
                    w.append(("pre1",))
                b.op("tensor", (lambda e, c=c: e.transpose(PT[c % 2][:, 0:128],
                                                           xn2n_sb[pr][:, c * 128:(c + 1) * 128], identb)),
                     label=("tr", t, c), waits=w)
                ew = [("tr", t, c)]
                if t % 4 == 0 and c == 0:
                    if grp >= 2:
                        ew.append(("xTfree", grp - 2))
                    if t == 0:
                        ew.append(("pre1",))
                dst = xn2nT[grp % 2][:, c * 512 + (t % 4) * 128: c * 512 + (t % 4) * 128 + 128]
                b.op("vector", TS(dst, PT[c % 2][:, 0:128], lng[:, c:c + 1], lnb[:, c:c + 1],
                                  ALU.mult, ALU.add),
                     label=[("trev", t, c)] + ([("trev8", t)] if c == 7 else []), waits=ew)

            if own:
                PF = [P2, P3]
                for c in range(8):
                    w = [("xn", t)]
                    if c == 0 and t == 0:
                        w.append(("pre1",))
                    if c >= 2:
                        w.append(("trfev", t, c - 2))
                    elif t >= 1:
                        if (t - 1) % 4 == 3:
                            w.append(("su", (t - 1) // 4, 14 + c))
                        else:
                            w.append(("trfev", t - 1, 6 + c))
                    b.op("tensor", (lambda e, c=c: e.transpose(PF[c % 2][:, 0:128],
                                                               xn_sb[pr][:, c * 128:(c + 1) * 128], identf)),
                         label=("trf", t, c), waits=w)
                    ew = [("trf", t, c)]
                    if c == 0 and t >= 1:
                        ew.append(("gmm", t - 1))
                    if t % 4 == 0 and c == 0 and grp >= 1:
                        ew.append(("xTst", grp - 1))
                    b.op("vector", CP(xnTf[:, c * 128:(c + 1) * 128], PF[c % 2][:, 0:128]), waits=ew)
                    b.op("vector", CP(xnT_b[:, c * 512 + (t % 4) * 128: c * 512 + (t % 4) * 128 + 128],
                                      PF[c % 2][:, 0:128]),
                         label=[("trfev", t, c)] + ([("xnTb", t)] if c == 7 else []))

            PV = [P0, P1]
            offs = [(E + g * 512, 512) for g in range(4)] + [(2 * E, S)]
            for g, (off, wdt) in enumerate(offs):
                gi = t * 5 + g
                bank = PV[gi % 2]
                for c in range(8):
                    w = []
                    if c == 0:
                        w.append(("trev8", t))
                        if gi >= 2:
                            pt_, pg_ = divmod(gi - 2, 5)
                            w.append(("sv", pt_, pg_) if pg_ < 4 else ("sz", pt_))
                        if t == 0 and g == 0:
                            w += [("lwinp",), ("lbin",)]
                    b.op("tensor", MM(bank[:, 0:wdt],
                                      xn2nT[grp % 2][:, c * 512 + (t % 4) * 128: c * 512 + (t % 4) * 128 + 128],
                                      Winp[:, c * F + off: c * F + off + wdt], c == 0, False), waits=w)
                lab = [("vps", t, g)]
                if g == 4 and t % 4 == 3 and not own:
                    lab.append(("xTfree", grp))
                b.op("tensor", MM(bank[:, 0:wdt], ones_bf[0:1, 0:128], binrow_bf[0:1, off:off + wdt],
                                  False, True), label=lab)
                if g < 4:
                    w = [("vps", t, g)]
                    if g == 0 and t >= 2:
                        w.append(("vst", t - 2))
                    b.op("scalar", ACT(v_sb[pr][:, g * 512:(g + 1) * 512], bank[:, 0:512], AF.Silu),
                         label=("sv", t, g), waits=w)
                else:
                    w = [("vps", t, g)]
                    if t >= 2:
                        w.append(("trz", t - 2))
                    b.op("scalar", ACT(z_sb[pr][:, :], bank[:, 0:S], AF.Silu), label=("sz", t), waits=w)
            b.dma("scalar", v_scr[t * 128:(t + 1) * 128, :], v_sb[pr], label=("vst", t),
                  waits=[("!", ("sv", t, 3))])

            if own and t % 4 == 3:
                PU = [P2, P3]
                for ec in range(16):
                    bank = PU[ec % 2]
                    for c in range(8):
                        w = []
                        if c == 0:
                            if ec <= 1:
                                w.append(("trfev", t, 6 + ec))
                            else:
                                w.append(("su", grp, ec - 2))
                            if ec == 0:
                                w.append(("trev8", t))
                        b.op("tensor", MM(bank[:, 0:512], Winp[:, c * F + ec * 128: c * F + (ec + 1) * 128],
                                          xn2nT[grp % 2][:, c * 512:(c + 1) * 512], c == 0, False), waits=w)
                    lab = [("ups", grp, ec)] + ([("xTfree", grp)] if ec == 15 else [])
                    b.op("tensor", MM(bank[:, 0:512], binrow_bf[0:1, ec * 128:(ec + 1) * 128],
                                      ones_bf[0:1, 0:512], False, True), label=lab)
                    w = [("ups", grp, ec)]
                    if ec >= 2:
                        w.append(("ust", grp, ec - 2))
                    elif grp >= 1:
                        w.append(("ust", grp - 1, 14 + ec))
                    b.op("scalar", ACT(u_sb[ec % 2][:, :], bank[:, 0:512], AF.Silu), label=("su", grp, ec),
                         waits=w)
                    b.dma("sync", uT_scr[ec * 128:(ec + 1) * 128, grp * 512:(grp + 1) * 512], u_sb[ec % 2],
                          label=("ust", grp, ec), waits=[("su", grp, ec)])
                b.dma("sync", xnT_scr.rearrange("(c p) t -> p c t", p=128)[:, :, grp * 512:(grp + 1) * 512],
                      xnT_b.rearrange("p (c t) -> p c t", c=8), label=("xTst", grp), waits=[("xnTb", t)])

            if own:
                po = (t % 2) * 8
                for c in range(8):
                    w = []
                    if c == 0:
                        w = [("trfev", t, 7), ("pre_gwT",)]
                        if t == 0:
                            w.append(("pre1",))
                        if t >= 2:
                            w.append(("g_ps", t - 2))
                    b.op("tensor", MM(PW[:, po:po + 8], xnTf[:, c * 128:(c + 1) * 128],
                                      gwT[:, c * 8:(c + 1) * 8], c == 0, False), waits=w)
                b.op("tensor", MM(PW[:, po:po + 8], ones_f[0:1, 0:128], gb[0:1, 0:8], False, True),
                     label=("gmm", t))
                lg = PW[:, po:po + 8]
                m1, m2, dd, w1v, w2v = st_[:, 0:1], st_[:, 1:2], st_[:, 2:3], st_[:, 3:4], st_[:, 4:5]
                eq1, msk, eq2, tmp = sq_scr[:, 0:8], sq_scr[:, 8:16], sq_scr[:, 16:24], sq_scr[:, 24:32]
                b.op("vector", (lambda e: e.reduce_max(m1, lg, axis=AX.X)), label=("gm1", t),
                     waits=[("gmm", t), ("ssq1v", t)])
                b.op("vector", TT(eq1, lg, m1.broadcast_to([128, 8]), ALU.is_equal),
                     label=("geq1", t), waits=[("!", ("gm1", t))])
                b.op("vector", STT(msk, eq1, -1e30, lg, ALU.mult, ALU.add),
                     label=("gmsk", t), waits=[("!", ("geq1", t))])
                b.op("vector", (lambda e: e.reduce_max(m2, msk, axis=AX.X)),
                     label=("gm2", t), waits=[("!", ("gmsk", t))])
                b.op("vector", TT(eq2, lg, m2.broadcast_to([128, 8]), ALU.is_equal), label=("g_ps", t),
                     waits=[("!", ("gm2", t))])
                b.op("vector", TT(dd, m2, m1, ALU.subtract), label=("gd", t))
                b.op("scalar", ACT(w1v, dd, AF.Sigmoid, scale=neg1_c[:, 0:1]), waits=[("gd", t)])
                b.op("scalar", ACT(w2v, dd, AF.Sigmoid), label=("gw", t))
                b.op("vector", TS(tmp, eq1, w1v, None, ALU.mult, ALU.bypass),
                     label=("gtmp", t), waits=[("gw", t)])
                b.op("vector", STT(W8[:, t * 8:(t + 1) * 8], eq2, w2v, tmp, ALU.mult, ALU.add),
                     label=("w8", t), waits=[("!", ("gtmp", t))])

        for t in range(NT):
            tile(t)
        tz = NT - 1
        ztranspose(tz)

        if debug:
            b.dma("sync", d_bin[:, :], binrow_bf[:, :], waits=[("kq", tz)])
            b.dma("sync", d_kT[:, :], kT[:, :], waits=[("kq", tz), ("w8", NOWN - 1)])
            b.dma("sync", d_qT[:, :], qT[:, :])
            b.dma("sync", d_w8[:, :], W8[:, :])

        P1LAST = [("trz", tz), ("kq", tz), ("sz", tz), ("vst", tz), ("ust", 7, 15), ("w8", NOWN - 1),
                  ("xTst", 7)]
        if stop_after < 2:
            b.wait("sync", *P1LAST)
            b.wait_count("sync", "QS")
            b.wait_count("sync", "QA")
            b.wait_count("sync", "QG")
            return

        b.dma("sync", Wout_sb.rearrange("p (c d) -> p c d", c=16),
              woutb.rearrange("(c p) d -> p c d", p=128), label=("lwout",), waits=[("wcout",)] + P1LAST)
        b.dma("scalar", boutr, boutb[:, :], label=("lbout",), waits=[("wcbout",)] + P1LAST)

        pairs = [(qb, s) for qb in range(NQB) for s in range(NSUP)]
        PB = [P2, P3, PW[:, 0:512], PW[:, 512:1024]]
        pidx = lambda qb, s: qb * NSUP + s

        def phaseA(qb, s):
            i = pidx(qb, s)
            PA = [P0, P1]
            for kk in range(KKS):
                kg = s * KKS + kk
                gi = i * KKS + kk
                w = []
                if gi >= 2:
                    w.append(("rl",) + divmod(gi - 2, KKS))
                if i == 0 and kk == 0:
                    w += P1LAST
                b.op("tensor", MM(PA[kk % 2][:, 0:512], kT[:, kg * 128:(kg + 1) * 128],
                                  qT[:, qb * 512:(qb + 1) * 512], True, True), label=("smm", i, kk), waits=w)
                w = [("smm", i, kk)]
                if gi >= 2:
                    w.append(("aTm",) + divmod(gi - 2, KKS))
                if i == 0 and kk == 0:
                    w += P1LAST
                b.op("scalar", ACT(rel_sb[kk % 2][:, :], PA[kk % 2][:, 0:512], AF.Relu), label=("rl", i, kk),
                     waits=w)
                w = [("rl", i, kk)]
                if kk == 0:
                    if i >= 2:
                        pq, ps_ = divmod(i - 2, NSUP)
                        w.append(("Wmm", pq, 3) if ps_ == NSUP - 1 else ("Bfree", i - 2))
                    if i == 0:
                        w += P1LAST
                b.op("vector", TT(aT[i % 2][:, kk * 512:(kk + 1) * 512], rel_sb[kk % 2], rel_sb[kk % 2],
                                  ALU.mult), label=("aTm", i, kk), waits=w)

        def phaseB(qb, s):
            i = pidx(qb, s)
            for p in range(4):
                for h in range(2):
                    j = (i * 4 + p) * 2 + h
                    w = []
                    if j >= 2:
                        w.append(("vread", j - 2))
                    if j == 0:
                        w += P1LAST
                    b.dma("sync", vbuf[j % 2].rearrange("p (c e) -> p c e", c=8),
                          v_scr.rearrange("(c p) e -> p c e", p=128)[:, s * KKS + h * 8: s * KKS + (h + 1) * 8,
                                                                    p * 512:(p + 1) * 512],
                          label=("lv", j), waits=w)
                for h in range(2):
                    j = (i * 4 + p) * 2 + h
                    for el in range(4):
                        for k8 in range(8):
                            w = []
                            if k8 == 0:
                                w.append(("lv", j))
                                if h == 0:
                                    if p >= 1:
                                        w.append(("fl", i, p - 1, el))
                                    elif i >= 1:
                                        w.append(("fl", i - 1, 3, el))
                                    if el == 0:
                                        w.append(("aTm", i, KKS - 1))
                                    if el == 2 and p == 0 and s == 0 and qb >= 1:
                                        w.append(("sev", qb - 1, 3))
                            lab = []
                            if el == 3 and k8 == 7:
                                lab.append(("vread", j))
                            if h == 1 and k8 == 7:
                                lab.append(("Bel", i, p, el))
                                if el == 3 and p == 3:
                                    lab.append(("Bfree", i))
                            b.op("tensor", MM(PB[el], vbuf[j % 2][:, k8 * 512 + el * 128: k8 * 512 + (el + 1) * 128],
                                              aT[i % 2][:, (h * 8 + k8) * 512:(h * 8 + k8 + 1) * 512],
                                              h == 0 and k8 == 0, h == 1 and k8 == 7),
                                 label=lab if lab else None, waits=w)
                for el in range(4):
                    ec = p * 4 + el
                    if s == 0:
                        b.op("vector", CP(acc[:, ec * 512:(ec + 1) * 512], PB[el]), label=("fl", i, p, el),
                             waits=[("Bel", i, p, el)])
                    else:
                        b.op("vector", TT(acc[:, ec * 512:(ec + 1) * 512], PB[el],
                                          acc[:, ec * 512:(ec + 1) * 512], ALU.add),
                             label=("fl", i, p, el), waits=[("Bel", i, p, el)])

        def blockend(qb):
            i7 = pidx(qb, NSUP - 1)
            mT = aT[i7 % 2]
            for ec in range(16):
                w = []
                if ec >= 2:
                    w.append(("mT", qb, ec - 2))
                elif qb >= 1:
                    w.append(("mT", qb - 1, 14 + ec))
                elif qb == 0 and ec <= 1:
                    w += P1LAST
                b.dma("scalar", u2_sb[ec % 2], uT_scr[ec * 128:(ec + 1) * 128, qb * 512:(qb + 1) * 512],
                      label=("lu", qb, ec), waits=w)
                b.op("vector", TT(mT[:, ec * 512:(ec + 1) * 512], u2_sb[ec % 2],
                                  acc[:, ec * 512:(ec + 1) * 512], ALU.mult), label=("mT", qb, ec),
                     waits=[("lu", qb, ec), ("fl", i7, 3, 3), ("Bfree", i7)])
            for tsub in range(4):
                tt = qb * 4 + tsub
                for ec in range(16):
                    w = [("mT", qb, ec)]
                    if ec == 0:
                        w.append(("lwout",))
                        if tsub >= 1:
                            w.append(("sev", qb, tsub - 1))
                        elif qb >= 1:
                            w.append(("sev", qb - 1, 3))
                    for half in range(2):
                        b.op("tensor", MM(PW[:, half * 512:(half + 1) * 512],
                                          mT[:, ec * 512 + tsub * 128: ec * 512 + tsub * 128 + 128],
                                          Wout_sb[:, ec * 1024 + half * 512: ec * 1024 + (half + 1) * 512],
                                          ec == 0, False), waits=w if half == 0 else [])
                for half in range(2):
                    b.op("tensor", MM(PW[:, half * 512:(half + 1) * 512], ones_bf[0:1, 0:128],
                                      boutr[0:1, half * 512:(half + 1) * 512], False, True),
                         label=("Wmm", qb, tsub) if half == 1 else None,
                         waits=[("lbout",)] if (qb == 0 and tsub == 0 and half == 0) else [])
                w = []
                if tt >= 2:
                    w.append(("sev", (tt - 2) // 4, (tt - 2) % 4))
                elif qb == 0:
                    w += P1LAST
                b.dma("scalar", xn2_sb[tsub % 2], xn_scr[tt * 128:(tt + 1) * 128, :], label=("lxn", qb, tsub),
                      waits=w)
                w = [("Wmm", qb, tsub), ("lxn", qb, tsub)]
                if tt >= 2:
                    w.append(("sst", (tt - 2) // 4, (tt - 2) % 4))
                b.op("vector", TT(s_sb[tsub % 2], PW[:, 0:1024], xn2_sb[tsub % 2], ALU.add),
                     label=("sev", qb, tsub), waits=w)
                b.dma("scalar", s_scr[tt * 128:(tt + 1) * 128, :], s_sb[tsub % 2], label=("sst", qb, tsub),
                      waits=[("sev", qb, tsub)])

        phaseA(*pairs[0])
        for n, (qb, s) in enumerate(pairs):
            if n + 1 < len(pairs):
                phaseA(*pairs[n + 1])
            phaseB(qb, s)
            if s == NSUP - 1:
                blockend(qb)

        P2LAST = [("Wmm", NQB - 1, 3), ("sev", NQB - 1, 3), ("sst", NQB - 1, 3),
                  ("rl", len(pairs) - 1, KKS - 1), ("lv", len(pairs) * 8 - 1)]
        if stop_after < 3:
            b.wait("sync", *P2LAST)
            b.wait_count("sync", "QS")
            b.wait_count("sync", "QA")
            b.wait_count("sync", "QG")
            return

        PS_PAIR = [(PW[:, 0:512], PW[:, 512:1024]), (P2, P3)]

        def unit(q, e_, hh):
            u = q * 16 + e_ * 2 + hh
            par = u % 2
            w = [("wc", e_, hh)]
            if u >= 2:
                w.append(("uend", u - 2))
            if u == 0:
                w += P2LAST
            b.dma("sync", w0s[par].rearrange("p (c h) -> p c h", c=8),
                  w0b.rearrange("e (c p) h -> e p c h", p=128)[e_][:, :, hh * 1024:(hh + 1) * 1024],
                  label=("lw0", u), waits=w)
            b.dma("sync", w1s[par].rearrange("p (c h) -> p c h", c=8),
                  w1b.rearrange("e (c p) h -> e p c h", p=128)[e_][:, :, hh * 1024:(hh + 1) * 1024],
                  label=("lw1", u))
            b.dma("sync", w2s[par].rearrange("p (c d) -> p c d", c=8),
                  w2b.rearrange("e (c p) d -> e p c d", p=128)[e_][:, hh * 8:(hh + 1) * 8, :],
                  label=("lw2", u))
            w = [("wcb2",)] if u < 2 else [("uend", u - 2)]
            if u == 0:
                w += P2LAST
            b.dma("gpsimd", b0c[par], b0_h[e_, :, hh * 8:(hh + 1) * 8], label=("lb0", u), waits=w)
            b.dma("gpsimd", b1c[par], b1_h[e_, :, hh * 8:(hh + 1) * 8], label=("lb1", u))
            if hh == 0:
                b.dma("gpsimd", b2r[e_ % 2], b2b[e_:e_ + 1, :], label=("lb2", q, e_))
            for tg in range(2):
                tgi = u * 2 + tg
                for hc in range(8):
                    for c in range(8):
                        w = []
                        if c == 0:
                            if hc >= 1:
                                w.append(("h0ev", u, tg, hc - 1))
                            elif tgi >= 1:
                                w.append(("h0ev", (tgi - 1) // 2, (tgi - 1) % 2, 7))
                            if hc == 0:
                                w += [("lw0", u), ("lw1", u), ("lxq", q)]
                                if u == 0 and tg == 0:
                                    w += P2LAST
                        b.op("tensor", MM(P0[:, 0:512],
                                          w0s[par][:, c * 1024 + hc * 128: c * 1024 + (hc + 1) * 128],
                                          xnT_q[:, c * 1024 + tg * 512: c * 1024 + tg * 512 + 512],
                                          c == 0, c == 7),
                             label=("h0", u, tg, hc) if c == 7 else None, waits=w)
                    for c in range(8):
                        w = []
                        if c == 0:
                            if hc >= 1:
                                w.append(("h1ev", u, tg, hc - 1))
                            elif tgi >= 1:
                                w.append(("h1ev", (tgi - 1) // 2, (tgi - 1) % 2, 7))
                        b.op("tensor", MM(P1[:, 0:512],
                                          w1s[par][:, c * 1024 + hc * 128: c * 1024 + (hc + 1) * 128],
                                          xnT_q[:, c * 1024 + tg * 512: c * 1024 + tg * 512 + 512],
                                          c == 0, c == 7),
                             label=("h1", u, tg, hc) if c == 7 else None, waits=w)
                    wv = [("h0", u, tg, hc), ("lb0", u)]
                    if hc >= 2:
                        wv.append(("act", u, tg, hc - 2))
                    elif tgi >= 1:
                        wv.append(("act", (tgi - 1) // 2, (tgi - 1) % 2, 6 + hc))
                    b.op("scalar", ACT(h0_sb[hc % 2][:, :], P0[:, 0:512], AF.Identity, bias=b0c[par][:, hc:hc + 1]),
                         label=("h0ev", u, tg, hc), waits=wv)
                    b.op("scalar", ACT(si_sb[hc % 2][:, :], P1[:, 0:512], AF.Silu, bias=b1c[par][:, hc:hc + 1]),
                         label=("h1ev", u, tg, hc), waits=[("h1", u, tg, hc), ("lb1", u)])
                    wv = [("h1ev", u, tg, hc)]
                    if hc == 0 and tgi >= 1:
                        wv.append(("w2", (tgi - 1) // 2, (tgi - 1) % 2, 3))
                    b.op("vector", TT(actT[:, hc * 512:(hc + 1) * 512], h0_sb[hc % 2], si_sb[hc % 2], ALU.mult),
                         label=("act", u, tg, hc), waits=wv)
                for tsub in range(4):
                    pp = PS_PAIR[tsub % 2]
                    ti = tgi * 4 + tsub
                    w = [("act", u, tg, 7), ("lw2", u)]
                    if ti >= 2:
                        w.append(("ae", (ti - 2) // 8, ((ti - 2) // 4) % 2, (ti - 2) % 4))
                    if u == 0 and tg == 0 and tsub <= 1:
                        w += P2LAST
                    last_is_mm = (hh == 1)
                    for hc in range(8):
                        for half in range(2):
                            lab = None
                            if last_is_mm and hc == 7 and half == 1:
                                lab = [("w2", u, tg, tsub)] + ([("uend", u)] if (tg == 1 and tsub == 3) else [])
                            b.op("tensor", MM(pp[half],
                                              actT[:, hc * 512 + tsub * 128: hc * 512 + tsub * 128 + 128],
                                              w2s[par][:, hc * 1024 + half * 512: hc * 1024 + (half + 1) * 512],
                                              hc == 0, hh == 1 and hc == 7),
                                 label=lab, waits=w if (hc == 0 and half == 0) else [])
                    if hh == 0:
                        for half in range(2):
                            lab = None
                            if half == 1:
                                lab = [("w2", u, tg, tsub)] + ([("uend", u)] if (tg == 1 and tsub == 3) else [])
                            b.op("tensor", MM(pp[half], ones_bf[0:1, 0:128],
                                              b2r[e_ % 2][0:1, half * 512:(half + 1) * 512], False, True),
                                 label=lab, waits=[("lb2", q, e_)] if half == 0 else [])
                    tt = q * 8 + tg * 4 + tsub
                    wcol = W8[:, tt * 8 + e_: tt * 8 + e_ + 1]
                    dst = acc_q[:, (tg * 4 + tsub) * 1024:(tg * 4 + tsub + 1) * 1024]
                    first = (e_ == 0 and hh == 0)
                    for half in range(2):
                        d2 = dst[:, half * 512:(half + 1) * 512]
                        lab = ("ae", u, tg, tsub) if half == 1 else None
                        if first:
                            b.op("vector", TS(d2, pp[half], wcol, None, ALU.mult, ALU.bypass), label=lab,
                                 waits=[("w2", u, tg, tsub)] if half == 0 else [])
                        else:
                            b.op("vector", STT(d2, pp[half], wcol, d2, ALU.mult, ALU.add), label=lab,
                                 waits=[("w2", u, tg, tsub)] if half == 0 else [])

        def quarter(q):
            w = [("uend", q * 16 - 1)] if q >= 1 else list(P2LAST)
            b.dma("sync", xnT_q.rearrange("p (c t) -> p c t", c=8),
                  xnT_scr.rearrange("(c p) t -> p c t", p=128)[:, :, q * 1024:(q + 1) * 1024],
                  label=("lxq", q), waits=w)
            for e_ in range(FE):
                for hh in range(2):
                    unit(q, e_, hh)
            ul = q * 16 + 15
            for ts8 in range(8):
                par = ts8 % 2
                tt = q * 8 + ts8
                w = [("ae", ul, ts8 // 4, ts8 % 4)]
                if ts8 >= 2:
                    w.append(("oc", q, ts8 - 2))
                elif q >= 1:
                    w.append(("oc", q - 1, 6 + ts8))
                b.op("scalar", ACT(g_sb[par][:, :], acc_q[:, ts8 * 1024:(ts8 + 1) * 1024], AF.Silu),
                     label=("g", q, ts8), waits=w)
                w = [("sg", q, ts8 - 2)] if ts8 >= 2 else ([("sg", q - 1, 6 + ts8)] if q >= 1 else [("Wmm", NQB - 1, 3)])
                b.dma("scalar", s3_sb[par], s_scr[tt * 128:(tt + 1) * 128, :], label=("ls3", q, ts8), waits=w)
                w = [("oc", q, ts8 - 2)] if ts8 >= 2 else ([("oc", q - 1, 6 + ts8)] if q >= 1 else [("Wmm", NQB - 1, 3)])
                b.dma("sync", x3_sb[par], xb[tt * 128:(tt + 1) * 128, :], label=("lx3", q, ts8), waits=w)
                b.op("vector", TT(s3_sb[par], s3_sb[par], g_sb[par], ALU.mult), label=("sg", q, ts8),
                     waits=[("g", q, ts8), ("ls3", q, ts8)])
                w = [("lx3", q, ts8), ("sg", q, ts8)]
                if ts8 >= 2:
                    w.append(("ost", q, ts8 - 2))
                elif q >= 1:
                    w.append(("ost", q - 1, 6 + ts8))
                b.op("vector", TT(x3_sb[par], s3_sb[par], x3_sb[par], ALU.add), label=("oc", q, ts8),
                     waits=w + [("!", ("sg", q, ts8))])
                b.dma("sync", O[tt * 128:(tt + 1) * 128, :], x3_sb[par], label=("ost", q, ts8),
                      waits=[("oc", q, ts8)])

        for q in range(4):
            quarter(q)

        b.wait("sync", ("ost", 3, 7))
        b.wait_count("sync", "QS")
        b.wait_count("sync", "QA")
        b.wait_count("sync", "QG")

    b.start_pass(COUNT)
    prog()
    b.final = dict(b.cnt)

    with nc.Block() as blk:
        @blk.sync
        def _(eng):
            b.start_pass("sync", eng)
            prog()

        @blk.gpsimd
        def _(eng):
            b.start_pass("gpsimd", eng)
            prog()

        @blk.tensor
        def _(eng):
            b.start_pass("tensor", eng)
            prog()

        @blk.scalar
        def _(eng):
            b.start_pass("scalar", eng)
            prog()

        @blk.vector
        def _(eng):
            b.start_pass("vector", eng)
            prog()

    return nc


def _prep_inputs(inp):
    f32 = lambda a: np.ascontiguousarray(np.asarray(a, dtype=np.float32))
    X = f32(inp["L0_attn_x"]).reshape(T, D)
    w0 = f32(inp["L0_ffn_w0"])
    w1 = f32(inp["L0_ffn_w1"])
    w2 = f32(inp["L0_ffn_w2"])
    com = {
        "w_in": f32(inp["L0_attn_W_in"]),
        "b_in": f32(inp["L0_attn_b_in"]).reshape(1, F),
        "lng": np.ascontiguousarray(f32(inp["L0_attn_ln_g"]).reshape(8, 128).T),
        "lnb": np.ascontiguousarray(f32(inp["L0_attn_ln_b"]).reshape(8, 128).T),
        "gq": f32(inp["L0_attn_gamma_q"]).reshape(128, 1),
        "bq": f32(inp["L0_attn_beta_q"]).reshape(128, 1),
        "gk": f32(inp["L0_attn_gamma_k"]).reshape(128, 1),
        "bk": f32(inp["L0_attn_beta_k"]).reshape(128, 1),
        "wout": f32(inp["L0_attn_W_out"]),
        "bout": f32(inp["L0_attn_b_out"]).reshape(1, D),
        "gwt": np.ascontiguousarray(f32(inp["L0_ffn_gate_w"]).T),
        "gb": f32(inp["L0_ffn_gate_b"]).reshape(1, FE),
        "w0t": np.ascontiguousarray(w0.transpose(0, 2, 1)),
        "w1t": np.ascontiguousarray(w1.transpose(0, 2, 1)),
        "w2t": np.ascontiguousarray(w2.transpose(0, 2, 1)),
        "b0": np.ascontiguousarray(f32(inp["L0_ffn_b0"]).reshape(FE, 16, 128).transpose(0, 2, 1)),
        "b1": np.ascontiguousarray(f32(inp["L0_ffn_b1"]).reshape(FE, 16, 128).transpose(0, 2, 1)),
        "b2": f32(inp["L0_ffn_b2"]),
    }
    in_maps = []
    for c in range(NC):
        bb = c // 4
        off = (c % 4) * TQ
        batch = X[bb * L:(bb + 1) * L]
        xbc = np.ascontiguousarray(
            np.concatenate([batch[off:off + TQ], batch[:off], batch[off + TQ:]], axis=0))
        m = dict(com)
        m["xb"] = xbc
        in_maps.append(m)
    return in_maps


def kernel(**inp):
    in_maps = _prep_inputs(inp)
    nc = build(stop_after=3, debug=DBG)
    res = run_bass_kernel_spmd(nc, in_maps, core_ids=list(range(NC)))
    out = np.concatenate([np.asarray(r["o"], dtype=np.float32) for r in res.results], axis=0)
    return out.reshape(NB, L, D)
